# version 15
# baseline (speedup 1.0000x reference)
import numpy as np
import ml_dtypes
from contextlib import ExitStack

import concourse.bass as bass
import concourse.mybir as mybir
from concourse.bass_utils import run_bass_kernel_spmd

F32 = mybir.dt.float32
BF16 = mybir.dt.bfloat16
AF = mybir.ActivationFunctionType
ALU = mybir.AluOpType
AX = mybir.AxisListType
BF = ml_dtypes.bfloat16

TT = 2048
DD = 2048
DC = 16
HQ = 16
KV = 4
HD = 128
SL = 4
TS = 512
TQ = 4
GH = 64
DS = 4
CAP = 50.0
SCALE = float(HD) ** -0.5
THC = 0.5009765625
EPS_QK = 1e-6
EPS_LN = 1e-5
EPS_FF = float(np.finfo(np.float32).eps)

ENGS = ("sync", "gpsimd", "tensor", "scalar", "vector")


class Pipe:

    COMPUTE = ("tensor", "scalar", "vector")

    def __init__(self, nc, ctx):
        self.nc = nc
        self.ctx = ctx
        self.sems = {}
        self.pos = {}
        self.stream = {e: [] for e in ENGS}
        self.writers = {}
        self.readers = {}

    def _sem(self, name):
        if name not in self.sems:
            self.sems[name] = self.ctx.enter_context(
                self.nc.semaphore("sem_" + name))
            self.pos[name] = 0
        return self.sems[name]

    ALIAS = {"H": ("qT", "TB0", "PB0", "PTB0", "TB1", "PB1", "PTB1", "OT"),
             "qT": ("H",), "TB0": ("H",), "PB0": ("H",), "PTB0": ("H",),
             "TB1": ("H",), "PB1": ("H",), "PTB1": ("H",), "OT": ("H",)}
    PSUMS = ("pA", "pB", "pC", "pT", "pO0", "pO1", "pO2", "pO3",
             "pW0", "pW1")

    def add(self, eng, fn, reads=(), writes=(), group=None):
        is_dma = eng in ("sync", "gpsimd")
        grp = group if group is not None else eng
        assert not (is_dma and grp == eng), \
            f"DMA step on {eng} needs an explicit slot/aggregate group"
        inc = 16 if is_dma else 1
        self._sem(grp)
        waits = {}

        def dep(d, kind):
            for sm, (pe, pv) in d.items():
                if pe == eng:
                    if kind == "raw":
                        if eng == "tensor":
                            continue
                    else:
                        if eng in self.COMPUTE:
                            continue
                waits[sm] = max(waits.get(sm, 0), pv)

        for r in reads:
            for rr in (r,) + self.ALIAS.get(r, ()):
                dep(self.writers.get(rr, {}), "raw")
            if r in self.PSUMS:
                dep(self.readers.get(r, {}), "war")
        for r in writes:
            for rr in (r,) + self.ALIAS.get(r, ()):
                dep(self.readers.get(rr, {}), "war")
                dep(self.writers.get(rr, {}), "waw")

        self.pos[grp] += inc
        after = self.pos[grp]
        for r in reads:
            self.readers.setdefault(r, {})[grp] = (eng, after)
        for r in writes:
            self.writers.setdefault(r, {})[grp] = (eng, after)
        self.stream[eng].append((fn, waits, grp))
        return after

    def drain_dmas(self):
        waits = {sm: self.pos[sm] for sm in self.sems
                 if sm not in ENGS and self.pos[sm] > 0}
        self.pos["vector"] = self.pos.get("vector", 0) + 1
        self._sem("vector")
        self.stream["vector"].append(
            (lambda e: e.engine_nop(), waits, "vector"))

    def emit_block(self):
        nc = self.nc
        sems = self.sems
        streams = {e: list(self.stream[e]) for e in ENGS}
        for e in ENGS:
            self.stream[e] = []
        with nc.Block() as block:
            def mk(name):
                def inner(e):
                    inc = 16 if name in ("sync", "gpsimd") else 1
                    watermark = {}
                    for fn, waits, grp in streams[name]:
                        for sm, pv in sorted(waits.items()):
                            if watermark.get(sm, 0) >= pv:
                                continue
                            e.wait_ge(sems[sm], pv)
                            watermark[sm] = pv
                        fn(e).then_inc(sems[grp], inc)
                return inner
            block.sync(mk("sync"))
            block.gpsimd(mk("gpsimd"))
            block.tensor(mk("tensor"))
            block.scalar(mk("scalar"))
            block.vector(mk("vector"))


def sap(handle, pitch, col0, nblk, stride, width=128):
    return bass.AP(handle, col0, [[pitch, 128], [stride, nblk], [1, width]])


def build(taps=(), nslices=SL, maxphase=7, m3sub=99):
    taps = set(taps)
    nc = bass.Bass("TRN2", target_bir_lowering=False)

    x_d = nc.dram_tensor("x", [TT, DD], F32, kind="ExternalInput").ap()
    wq_d = nc.dram_tensor("wq", [DD, DD], F32, kind="ExternalInput").ap()
    wk_d = nc.dram_tensor("wk", [512, DD], F32, kind="ExternalInput").ap()
    wv_d = nc.dram_tensor("wv", [512, DD], F32, kind="ExternalInput").ap()
    wo_d = nc.dram_tensor("wo", [DD, DD], F32, kind="ExternalInput").ap()
    qg_d = nc.dram_tensor("qg", [HD], F32, kind="ExternalInput").ap()
    kg_d = nc.dram_tensor("kg", [HD], F32, kind="ExternalInput").ap()
    wgu_d = nc.dram_tensor("wgu", [16384, DD], F32, kind="ExternalInput").ap()
    wp_d = nc.dram_tensor("wp", [DD, 8192], F32, kind="ExternalInput").ap()
    out_d = nc.dram_tensor("out", [TT, DD], F32, kind="ExternalOutput").ap()

    wqT_s = nc.dram_tensor("wqT_s", [16 * 128, 2048], BF16, kind="Internal").ap()
    wguT_s = nc.dram_tensor("wguT_s", [128 * 128, 2048], BF16, kind="Internal").ap()
    woT_s = nc.dram_tensor("woT_s", [4 * 16 * 128, 512], BF16, kind="Internal").ap()
    wpT_s = nc.dram_tensor("wpT_s", [4 * 64 * 128, 512], BF16, kind="Internal").ap()

    ident_np = np.eye(128, dtype=BF)
    identf_np = np.eye(128, dtype=np.float32)
    ones_np = np.ones((128, 128), dtype=BF)
    tri_np = np.where(np.arange(128)[None, :] <= np.arange(128)[:, None],
                      np.float32(9.0), np.float32(-1e5)).astype(np.float32)
    ident_d = nc.inline_tensor(ident_np, name="identc").ap()
    identf_d = nc.inline_tensor(identf_np, name="identfc").ap()
    ones_d = nc.inline_tensor(ones_np, name="onesc").ap()
    tri_d = nc.inline_tensor(tri_np, name="tric").ap()

    dbg = {}

    def tapt(name, shape, dt):
        ap = nc.dram_tensor("tap_" + name, list(shape), dt,
                            kind="ExternalOutput").ap()
        dbg[name] = ap
        return ap

    with ExitStack() as ctx:
        pp = Pipe(nc, ctx)

        sb = lambda n_, s_, d_: ctx.enter_context(nc.sbuf_tensor(n_, s_, d_))
        ident_sb = sb("ident_sb", [128, 128], BF16)
        identf_sb = sb("identf_sb", [128, 128], F32)
        ones_sb = sb("ones_sb", [128, 128], BF16)
        tri_sb = sb("tri_sb", [128, 128], F32)
        gq_sb = sb("gq_sb", [128, 1], F32)
        gk_sb = sb("gk_sb", [128, 1], F32)
        eps_ln = sb("eps_ln", [128, 1], F32)
        eps_qk = sb("eps_qk", [128, 1], F32)
        eps_ff = sb("eps_ff", [128, 1], F32)
        wkT_sb = sb("wkT_sb", [128, 8192], BF16)
        wvT_sb = sb("wvT_sb", [128, 8192], BF16)

        CONST = "consts"
        pp.add("sync", lambda e: e.dma_start(out=ident_sb[:], in_=ident_d),
               writes=[CONST], group="g_const")
        pp.add("sync", lambda e: e.dma_start(out=identf_sb[:], in_=identf_d),
               writes=[CONST], group="g_const")
        pp.add("sync", lambda e: e.dma_start(out=ones_sb[:], in_=ones_d),
               writes=[CONST], group="g_const")
        pp.add("sync", lambda e: e.dma_start(out=tri_sb[:], in_=tri_d),
               writes=[CONST], group="g_const")
        pp.add("sync", lambda e: e.dma_start(
            out=gq_sb[:], in_=qg_d.rearrange("(p o) -> p o", o=1)), writes=[CONST], group="g_const")
        pp.add("sync", lambda e: e.dma_start(
            out=gk_sb[:], in_=kg_d.rearrange("(p o) -> p o", o=1)), writes=[CONST], group="g_const")
        pp.add("vector", lambda e: e.memset(eps_ln[:], EPS_LN), writes=[CONST])
        pp.add("vector", lambda e: e.memset(eps_qk[:], EPS_QK), writes=[CONST])
        pp.add("vector", lambda e: e.memset(eps_ff[:], EPS_FF), writes=[CONST])

        with ExitStack() as wctx:
            wsb = lambda n_, s_, d_: wctx.enter_context(nc.sbuf_tensor(n_, s_, d_))
            NWS = 3
            wnat = [wsb(f"wnat{i}", [128, 2048], BF16) for i in range(NWS)]
            what = [wsb(f"what{i}", [128, 2048], BF16) for i in range(NWS)]
            u_t = wsb("u_t", [128, 2048], BF16)
            sg_t = [wsb(f"sg_t{i}", [128, 2048], BF16) for i in range(NWS)]
            aw_t = [wsb(f"aw_t{i}", [128, 2048], BF16) for i in range(NWS)]
            gsum = wsb("gsum", [128, 128], F32)
            sc_bf = wsb("sc_bf", [128, 128], BF16)
            th_f = wsb("th_f", [128, 128], F32)
            pstage = [wsb(f"pstage{i}", [128, 2048], BF16) for i in range(NWS)]
            dstage = wsb("dstage", [128, 16 * 512], BF16)
            pW = [wctx.enter_context(nc.psum_tensor(f"pW{i}", [128, 1024], BF16))
                  for i in range(2)]
            rctr = [0]

            def quant_steps(wcols, sl):
                G = wcols // 64
                WN = f"wnat{sl}"
                WH = f"what{sl}"
                w3 = wnat[sl][:, 0:wcols].rearrange("p (g k) -> p g k", k=64)
                thb = th_f[:, 0:G].unsqueeze(2).broadcast_to([128, G, 64])
                scb = sc_bf[:, 0:G].unsqueeze(2).broadcast_to([128, G, 64])
                u3 = u_t[:, 0:wcols].rearrange("p (g k) -> p g k", k=64)
                s3 = sg_t[sl][:, 0:wcols].rearrange("p (g k) -> p g k", k=64)
                a3 = aw_t[sl][:, 0:wcols].rearrange("p (g k) -> p g k", k=64)
                h3 = what[sl][:, 0:wcols].rearrange("p (g k) -> p g k", k=64)
                pp.add("scalar", (lambda sl=sl, wcols=wcols: lambda e: e.activation(
                    aw_t[sl][:, 0:wcols], wnat[sl][:, 0:wcols], AF.Abs))(),
                    reads=[WN], writes=[f"awt{sl}"])
                pp.add("scalar", (lambda sl=sl, wcols=wcols: lambda e: e.activation(
                    sg_t[sl][:, 0:wcols], wnat[sl][:, 0:wcols], AF.Sign))(),
                    reads=[WN], writes=[f"sgt{sl}"])
                pp.add("vector", lambda e: e.tensor_reduce(
                    gsum[:, 0:G], w3, AX.X, ALU.add, apply_absolute_value=True),
                    reads=[WN], writes=["gsum"])
                pp.add("vector", lambda e: e.tensor_scalar(
                    sc_bf[:, 0:G], gsum[:, 0:G], 1.0 / 64.0, 1e-8,
                    ALU.mult, ALU.max),
                    reads=["gsum"], writes=["scbf"])
                pp.add("vector", lambda e: e.tensor_scalar_mul(
                    th_f[:, 0:G], sc_bf[:, 0:G], THC),
                    reads=["scbf"], writes=["thf"])
                pp.add("vector", (lambda sl=sl: lambda e: e.tensor_tensor(
                    u3, a3, thb, ALU.is_gt))(),
                    reads=[f"awt{sl}", "thf"], writes=["ut"])
                pp.add("vector", (lambda sl=sl: lambda e: e.tensor_tensor(
                    u3, u3, s3, ALU.mult))(),
                    reads=[f"ut", f"sgt{sl}"], writes=["ut"])
                pp.add("vector", lambda e: e.tensor_tensor(h3, u3, scb, ALU.mult),
                       reads=["ut", "scbf"], writes=[WH])

            def wclass(src, ntiles, wcols, quant, mode, dst=None, dstres=None,
                       dst_sb=None, dst_pitch=None, colmap=None, sbres=None,
                       srcfn=None, wrowfn=None):
                nb = wcols // 128
                for i in range(ntiles):
                    sl = i % NWS
                    WN = f"wnat{sl}"
                    WH = f"what{sl}"
                    PS = f"pstage{sl}"
                    if srcfn is None:
                        sfn = (lambda i=i: lambda: src[i * 128:(i + 1) * 128, :])()
                    else:
                        sfn = (lambda i=i: lambda: srcfn(i))()
                    pp.add("gpsimd", (lambda sfn=sfn, sl=sl, wcols=wcols:
                                      lambda e: e.dma_start(
                        out=wnat[sl][:, 0:wcols], in_=sfn()))(), writes=[WN],
                        group=f"g_{WN}")
                    if quant:
                        quant_steps(wcols, sl)
                    srcT = what[sl] if quant else wnat[sl]
                    SR = WH if quant else WN
                    for r in range((nb + 7) // 8):
                        n = min(8, nb - r * 8)

                        pwp = rctr[0] % 2
                        rctr[0] += 1

                        def trn(e, r=r, n=n, srcT=srcT, pwp=pwp):
                            for b in range(n):
                                blk = r * 8 + b
                                ins = e.transpose(
                                    pW[pwp][:, b * 128:(b + 1) * 128],
                                    srcT[:, blk * 128:(blk + 1) * 128],
                                    ident_sb[:])
                            return ins
                        pp.add("tensor", trn, reads=[SR, CONST],
                               writes=[f"pW{pwp}"])

                        if mode == "panel":
                            dfn = (lambda r=r, n=n, sl=sl: lambda: (
                                pstage[sl][:, r * 1024:r * 1024 + n * 128]))()
                            wres = PS
                        elif mode == "sbuf":
                            dfn = (lambda i=i, r=r, n=n: lambda: sap(
                                dst_sb, dst_pitch, colmap(i, r), n,
                                colmap("stride", 0)))()
                            wres = sbres
                        else:
                            dfn = (lambda i=i, r=r, n=n: lambda: sap(
                                dstage, 16 * 512, (r * 8) * 512 + (i % 4) * 128,
                                n, 512))()
                            wres = f"dstage{r}"
                        pp.add("scalar", (lambda dfn=dfn, n=n, pwp=pwp:
                                          lambda e: e.copy(
                            dfn(), pW[pwp][:, 0:n * 128]))(),
                            reads=[f"pW{pwp}"], writes=[wres])
                        if mode == "dsgroup" and i % 4 == 3:
                            for hb in range(r * 8, r * 8 + n):
                                row = (wrowfn(i, hb) if wrowfn
                                       else (i // 4) * nb + hb)
                                pp.add("sync", (lambda row=row, hb=hb:
                                                lambda e: e.dma_start(
                                    out=dst[row * 128:(row + 1) * 128, :],
                                    in_=dstage[:, hb * 512:(hb + 1) * 512]))(),
                                    reads=[f"dstage{r}"], writes=[dstres],
                                    group="g_" + dstres)
                    if mode == "panel":
                        pp.add("sync", (lambda i=i, sl=sl, wcols=wcols:
                                        lambda e: e.dma_start(
                            out=dst[i * 128:(i + 1) * 128, :],
                            in_=pstage[sl][:, 0:wcols]))(),
                            reads=[PS], writes=[dstres], group=f"g_ws{sl}")

            def kmap(i, r=None):
                if i == "stride":
                    return 128
                return i * 2048 + (r * 8) * 128

            def vmap(i, r=None):
                if i == "stride":
                    return 512
                return (r * 8) * 512 + i * 128

            wclass(wk_d, 4, 2048, False, "sbuf", dst_sb=wkT_sb, dst_pitch=8192,
                   colmap=kmap, sbres="wkTsb")
            wclass(wv_d, 4, 2048, False, "sbuf", dst_sb=wvT_sb, dst_pitch=8192,
                   colmap=vmap, sbres="wvTsb")
            wclass(wq_d, 16, 2048, False, "panel", dst=wqT_s, dstres="wqTs")
            wclass(wgu_d, 128, 2048, True, "panel", dst=wguT_s, dstres="wguTs")
            wclass(wo_d, 16, 2048, False, "dsgroup", dst=woT_s, dstres="woTs")
            def wp_src(i):
                g, cc, ii = i // 16, (i // 4) % 4, i % 4
                row = (g * 4 + ii) * 128
                return wp_d[row:row + 128, cc * 2048:(cc + 1) * 2048]

            def wp_row(i, hb):
                g, cc = i // 16, (i // 4) % 4
                return g * 64 + cc * 16 + hb
            wclass(None, 64, 2048, True, "dsgroup", dst=wpT_s, dstres="wpTs",
                   srcfn=wp_src, wrowfn=wp_row)

            for nm, src_, shp, res in [
                    ("wguT0", lambda: wguT_s[0:128, :], [128, 2048], "wguTs"),
                    ("wkT", lambda: wkT_sb[:], [128, 8192], "wkTsb"),
                    ("wvT", lambda: wvT_sb[:], [128, 8192], "wvTsb"),
                    ("wpT0", lambda: wpT_s[0:128, :], [128, 512], "wpTs"),
                    ("woT0", lambda: woT_s[0:128, :], [128, 512], "woTs"),
                    ("wqT0", lambda: wqT_s[0:128, :], [128, 2048], "wqTs")]:
                if nm in taps:
                    tdst = tapt(nm, shp, BF16)
                    pp.add("gpsimd", (lambda tdst=tdst, src_=src_:
                                      lambda e: e.dma_start(
                                          out=tdst, in_=src_()))(),
                           reads=[res], writes=["OUT"], group="g_tap")

            pp.drain_dmas()
            pp.emit_block()

        xs_sb = sb("xs_sb", [128, 4 * 2048], F32)
        xnT_sb = sb("xnT_sb", [128, DC * 512], BF16)
        knT_sb = sb("knT_sb", [128, KV * 2048], BF16)
        v_sb = sb("v_sb", [128, 16 * 512], BF16)
        big_sb = sb("big_sb", [128, 32768], BF16)
        sqs_sb = sb("sqs_sb", [128, 512], BF16)
        rbb_sb = sb("rbb_sb", [128, 512], F32)
        rkb_sb = sb("rkb_sb", [128, 512], F32)
        rqc_sb = sb("rqc_sb", [128, 64], F32)
        mst_sb = sb("mst_sb", [128, 32], F32)
        stat_sb = sb("stat_sb", [128, 8], F32)
        invc_sb = sb("invc_sb", [128, 4], F32)
        As_sb = [sb("As0", [128, 512], F32)]
        hsq_sb = [sb("hsq0", [128, 512], BF16)]
        outb_sb = [sb("outb0", [128, 512], F32)]
        NWST = 4
        wst_sb = [sb(f"wst{i}", [128, 2048], BF16) for i in range(NWST)]
        wring = [0]

        def next_wst():
            r_ = wring[0] % NWST
            wring[0] += 1
            return r_

        qT_sb = big_sb[:, 0:8192]
        OT_sb = big_sb[:, 8192:16384]
        TBp = [big_sb[:, 16384:20480].bitcast(F32),
               big_sb[:, 24576:28672].bitcast(F32)]
        PBp = [big_sb[:, 20480:22528], big_sb[:, 28672:30720]]
        PTBp = [big_sb[:, 22528:24576], big_sb[:, 30720:32768]]

        p_out = ctx.enter_context(nc.psum_tensor("p_out", [128, 2048], F32))
        pA = ctx.enter_context(nc.psum_tensor("pA", [128, 512], F32))
        pB = ctx.enter_context(nc.psum_tensor("pB", [128, 512], F32))
        pC = ctx.enter_context(nc.psum_tensor("pC", [128, 512], F32))
        pT = ctx.enter_context(nc.psum_tensor("pT", [128, 1024], BF16))
        pS = [pA, pB]
        pO = [p_out[:, 0:128], p_out[:, 512:640]]

        for s in range(nslices):
            tok0 = s * TS
            knres = [f"knT{ss}" for ss in range(s + 1)]
            vres = [f"v{ss}" for ss in range(s + 1)]

            for t in range(TQ):
                XS = f"xs{t}"
                xsl = xs_sb[:, t * 2048:(t + 1) * 2048]
                pp.add("gpsimd", (lambda t=t, tok0=tok0: lambda e: e.dma_start(
                    out=xs_sb[:, t * 2048:(t + 1) * 2048],
                    in_=x_d[tok0 + t * 128:tok0 + (t + 1) * 128, :]))(),
                    writes=[XS], group=f"g_xs{t}")
                par = t % 2
                pp.add("scalar", (lambda xsl=xsl, par=par: lambda e: e.activation(
                    TBp[par][:, :], xsl, AF.Square, accum_out=stat_sb[:, 1:2]))(),
                    reads=[XS], writes=[f"TB{par}", "stat"])
                pp.add("vector", (lambda xsl=xsl: lambda e: e.reduce_sum(
                    stat_sb[:, 0:1], xsl, AX.X))(), reads=[XS], writes=["stat"])

                def stats(e):
                    e.tensor_scalar_mul(stat_sb[:, 2:3], stat_sb[:, 0:1],
                                        1.0 / 2048.0)
                    return e.tensor_scalar_mul(stat_sb[:, 5:6], stat_sb[:, 1:2],
                                               1.0 / 2048.0)
                pp.add("vector", stats, reads=["stat"], writes=["stat"])
                pp.add("vector", lambda e: e.tensor_tensor(
                    stat_sb[:, 6:7], stat_sb[:, 2:3], stat_sb[:, 2:3], ALU.mult),
                    reads=["stat"], writes=["stat"])
                pp.add("vector", lambda e: e.tensor_tensor(
                    stat_sb[:, 3:4], stat_sb[:, 5:6], stat_sb[:, 6:7],
                    ALU.subtract), reads=["stat"], writes=["stat"])
                pp.add("scalar", lambda e: e.activation(
                    stat_sb[:, 4:5], stat_sb[:, 3:4], AF.Sqrt, bias=eps_ln[:]),
                    reads=["stat", CONST], writes=["stat"])
                pp.add("vector", lambda e: e.reciprocal(stat_sb[:, 7:8],
                                                        stat_sb[:, 4:5]),
                       reads=["stat"], writes=["stat"])
                pp.add("vector", (lambda xsl=xsl, par=par: lambda e: e.tensor_scalar(
                    TBp[par].bitcast(BF16)[:, 0:2048], xsl, stat_sb[:, 2:3],
                    stat_sb[:, 7:8], ALU.subtract, ALU.mult))(),
                    reads=[XS, "stat"], writes=[f"TB{par}"])
                for r in range(2):
                    def trn(e, r=r, par=par):
                        xnb = TBp[par].bitcast(BF16)
                        for b in range(8):
                            ins = e.transpose(pT[:, b * 128:(b + 1) * 128],
                                              xnb[:, (r * 8 + b) * 128:
                                                  (r * 8 + b + 1) * 128],
                                              ident_sb[:])
                        return ins
                    pp.add("tensor", trn, reads=[f"TB{par}", CONST], writes=["pT"])
                    pp.add("scalar", (lambda t=t, r=r: lambda e: e.copy(
                        sap(xnT_sb, DC * 512, (r * 8) * 512 + t * 128, 8, 512),
                        pT[:, 0:1024]))(), reads=["pT"], writes=["xnT"])

            if maxphase < 2:
                continue
            for kvc in range(KV):
                def kproj(e, kvc=kvc):
                    for dc in range(DC):
                        ins = e.matmul(pA[:],
                                       wkT_sb[:, kvc * 2048 + dc * 128:
                                              kvc * 2048 + (dc + 1) * 128],
                                       xnT_sb[:, dc * 512:(dc + 1) * 512],
                                       start=(dc == 0), stop=(dc == DC - 1))
                    return ins
                pp.add("tensor", kproj, reads=["wkTsb", "xnT"], writes=["pA"])
                pp.add("scalar", lambda e: e.activation(sqs_sb[:], pA[:],
                                                        AF.Square),
                       reads=["pA"], writes=["sqs"])
                pp.add("tensor", lambda e: e.matmul(pB[:], ones_sb[:], sqs_sb[:],
                                                    start=True, stop=True),
                       reads=["sqs", CONST], writes=["pB"])
                pp.add("scalar", lambda e: e.activation(
                    rbb_sb[:], pB[:], AF.Sqrt, bias=eps_qk[:], scale=1.0 / 128.0),
                    reads=["pB", CONST], writes=["rbb"])
                pp.add("vector", lambda e: e.reciprocal(rkb_sb[:], rbb_sb[:]),
                       reads=["rbb"], writes=["rkb"])
                pp.add("vector", (lambda kvc=kvc, tok0=tok0: lambda e:
                                  e.scalar_tensor_tensor(
                    knT_sb[:, kvc * 2048 + tok0:kvc * 2048 + tok0 + TS],
                    pA[:], gk_sb[:], rkb_sb[:], ALU.mult, ALU.mult))(),
                    reads=["pA", "rkb", CONST], writes=[f"knT{s}"])
            for t in range(TQ):
                def vproj(e, t=t):
                    for dc in range(DC):
                        ins = e.matmul(pC[:],
                                       xnT_sb[:, dc * 512 + t * 128:
                                              dc * 512 + (t + 1) * 128],
                                       wvT_sb[:, dc * 512:(dc + 1) * 512],
                                       start=(dc == 0), stop=(dc == DC - 1))
                    return ins
                pp.add("tensor", vproj, reads=["xnT", "wvTsb"], writes=["pC"])
                pp.add("vector", (lambda s=s, t=t: lambda e: e.tensor_copy(
                    v_sb[:, (s * TQ + t) * 512:(s * TQ + t + 1) * 512], pC[:]))(),
                    reads=["pC"], writes=[f"v{s}"])

            if maxphase < 3:
                continue
            for hq in range(HQ):
                wr = next_wst()
                WR = f"wst{wr}"
                pp.add("sync", (lambda hq=hq, wr=wr: lambda e: e.dma_start(
                    out=wst_sb[wr][:], in_=wqT_s[hq * 128:(hq + 1) * 128, :]))(),
                    reads=["wqTs"], writes=[WR], group=f"g_wst{wr}")

                if m3sub < 1:
                    continue
                def qproj(e, wr=wr):
                    for dc in range(DC):
                        ins = e.matmul(pA[:],
                                       wst_sb[wr][:, dc * 128:(dc + 1) * 128],
                                       xnT_sb[:, dc * 512:(dc + 1) * 512],
                                       start=(dc == 0), stop=(dc == DC - 1))
                    return ins
                pp.add("tensor", qproj, reads=[WR, "xnT"], writes=["pA"])
                if m3sub < 2:
                    continue
                pp.add("vector", (lambda hq=hq: lambda e: e.tensor_scalar_mul(
                    qT_sb[:, hq * 512:(hq + 1) * 512], pA[:], gq_sb[:]))(),
                    reads=["pA", CONST], writes=["qT"])
                if m3sub < 3:
                    continue
                pp.add("scalar", lambda e: e.activation(sqs_sb[:], pA[:],
                                                        AF.Square),
                       reads=["pA"], writes=["sqs"])
                if m3sub < 4:
                    continue
                pp.add("tensor", lambda e: e.matmul(pB[:], ones_sb[:], sqs_sb[:],
                                                    start=True, stop=True),
                       reads=["sqs", CONST], writes=["pB"])
                if m3sub < 5:
                    continue
                pp.add("scalar", lambda e: e.activation(
                    rbb_sb[:], pB[:], AF.Sqrt, bias=eps_qk[:], scale=1.0 / 128.0),
                    reads=["pB", CONST], writes=["rbb"])
                if m3sub < 6:
                    continue
                pp.add("vector", lambda e: e.reciprocal(rkb_sb[:], rbb_sb[:]),
                       reads=["rbb"], writes=["rkb"])
                if m3sub < 7:
                    continue

                def rqtr(e):
                    for t in range(TQ):
                        ins = e.transpose(pC[:, t * 128:(t + 1) * 128],
                                          rkb_sb[:, t * 128:(t + 1) * 128],
                                          identf_sb[:])
                    return ins
                pp.add("tensor", rqtr, reads=["rkb", CONST], writes=["pC"])

                def rqcol(e, hq=hq):
                    for t in range(TQ):
                        ins = e.tensor_scalar_mul(
                            rqc_sb[:, hq * 4 + t:hq * 4 + t + 1],
                            pC[:, t * 128:t * 128 + 1], SCALE / CAP)
                    return ins
                pp.add("vector", rqcol, reads=["pC"], writes=["rqc"])

            if s == 0 and "rqchain" in taps:
                t_s = tapt("sqs", [128, 512], BF16)
                pp.add("gpsimd", lambda e: e.dma_start(out=t_s, in_=sqs_sb[:]),
                       reads=["sqs"], writes=["OUT"], group="g_tap")
                t_r1 = tapt("rbb", [128, 512], F32)
                pp.add("gpsimd", lambda e: e.dma_start(out=t_r1, in_=rbb_sb[:]),
                       reads=["rbb"], writes=["OUT"], group="g_tap")
                t_r2 = tapt("rkb", [128, 512], F32)
                pp.add("gpsimd", lambda e: e.dma_start(out=t_r2, in_=rkb_sb[:]),
                       reads=["rkb"], writes=["OUT"], group="g_tap")
                t_pc = tapt("pCe", [128, 512], F32)
                pp.add("vector", lambda e: e.tensor_copy(rbb_sb[:], pC[:]),
                       reads=["pC"], writes=["rbb"])
                pp.add("gpsimd", lambda e: e.dma_start(out=t_pc, in_=rbb_sb[:]),
                       reads=["rbb"], writes=["OUT"], group="g_tap")
            if s == 0:
                for nm, srcfn, shp, dt_, res in [
                        ("xnT", lambda: xnT_sb[:], [128, DC * 512], BF16, "xnT"),
                        ("knT", lambda: knT_sb[:], [128, KV * 2048], BF16, "knT0"),
                        ("v", lambda: v_sb[:], [128, 16 * 512], BF16, "v0"),
                        ("qT", lambda: qT_sb, [128, 8192], BF16, "qT"),
                        ("rqc", lambda: rqc_sb[:], [128, 64], F32, "rqc")]:
                    if nm in taps:
                        tdst = tapt(nm, shp, dt_)
                        pp.add("gpsimd", (lambda tdst=tdst, srcfn=srcfn:
                                          lambda e: e.dma_start(
                                              out=tdst, in_=srcfn()))(),
                               reads=[res], writes=["OUT"], group="g_tap")

            if maxphase < 4:
                continue
            its = []
            for t in range(TQ):
                tqg = s * TQ + t
                for h in range(HQ):
                    its.append((t, h, tqg))

            def m4_meta(k):
                t, h, tqg = its[k]
                ncols = (tqg + 1) * 128
                return dict(t=t, h=h, g=h // 4, tqg=tqg, ncols=ncols,
                            J=(ncols + 511) // 512, ntk=tqg + 1, par=k % 2,
                            mo=16 * (k % 2))

            def m4_F(k):
                m = m4_meta(k)
                t, h, g, J, ncols, par = (m["t"], m["h"], m["g"], m["J"],
                                          m["ncols"], m["par"])
                TB = TBp[par]
                RTB = f"TB{par}"
                for j in range(J):
                    nj = min(512, ncols - j * 512)
                    PSR = f"p{'AB'[j % 2]}"
                    pp.add("tensor", (lambda h=h, t=t, g=g, j=j, nj=nj:
                                      lambda e: e.matmul(
                        pS[j % 2][:, 0:nj],
                        qT_sb[:, h * 512 + t * 128:h * 512 + (t + 1) * 128],
                        knT_sb[:, g * 2048 + j * 512:g * 2048 + j * 512 + nj],
                        start=True, stop=True))(),
                        reads=["qT"] + knres, writes=[PSR])
                    pp.add("scalar", (lambda h=h, t=t, j=j, nj=nj, TB=TB:
                                      lambda e: e.activation(
                        TB[:, j * 512:j * 512 + nj], pS[j % 2][:, 0:nj],
                        AF.Tanh,
                        scale=rqc_sb[:, h * 4 + t:h * 4 + t + 1]))(),
                        reads=[PSR, "rqc"], writes=[RTB])

            def m4_B1(k):
                m = m4_meta(k)
                ncols, par, mo = m["ncols"], m["par"], m["mo"]
                TB = TBp[par]
                RTB, RMS = f"TB{par}", f"mst{par}"
                pp.add("vector", (lambda ncols=ncols, TB=TB:
                                  lambda e: e.tensor_tensor(
                    TB[:, ncols - 128:ncols], TB[:, ncols - 128:ncols],
                    tri_sb[:], ALU.min))(),
                    reads=[RTB, CONST], writes=[RTB])
                pp.add("vector", (lambda ncols=ncols, TB=TB, mo=mo:
                                  lambda e: e.reduce_max(
                    mst_sb[:, mo + 4:mo + 5], TB[:, 0:ncols], AX.X))(),
                    reads=[RTB], writes=[RMS])
                pp.add("vector", (lambda mo=mo: lambda e: e.tensor_scalar_mul(
                    mst_sb[:, mo + 5:mo + 6], mst_sb[:, mo + 4:mo + 5], -CAP))(),
                    reads=[RMS], writes=[RMS])

            def m4_B2(k):
                m = m4_meta(k)
                ncols, J, par, mo = m["ncols"], m["J"], m["par"], m["mo"]
                TB, PB = TBp[par], PBp[par]
                RTB, RPB = f"TB{par}", f"PB{par}"
                RMS, RMSS = f"mst{par}", f"msts{par}"
                for j in range(J):
                    nj = min(512, ncols - j * 512)
                    pp.add("scalar", (lambda j=j, nj=nj, TB=TB, PB=PB, mo=mo:
                                      lambda e: e.activation(
                        PB[:, j * 512:j * 512 + nj],
                        TB[:, j * 512:j * 512 + nj],
                        AF.Exp, bias=mst_sb[:, mo + 5:mo + 6], scale=CAP,
                        accum_out=mst_sb[:, mo + 8 + j:mo + 9 + j]))(),
                        reads=[RTB, RMS], writes=[RPB, RMSS])

            def m4_B3(k):
                m = m4_meta(k)
                ncols, J, par, mo = m["ncols"], m["J"], m["par"], m["mo"]
                PB = PBp[par]
                RPB = f"PB{par}"
                RMSS, RMSS2 = f"msts{par}", f"msts2{par}"
                if J > 1:
                    pp.add("vector", (lambda J=J, mo=mo: lambda e: e.reduce_sum(
                        mst_sb[:, mo + 12:mo + 13],
                        mst_sb[:, mo + 8:mo + 8 + J], AX.X))(),
                        reads=[RMSS], writes=[RMSS2])
                    scol = 12
                else:
                    scol = 8
                pp.add("vector", (lambda mo=mo, scol=scol: lambda e: e.reciprocal(
                    mst_sb[:, mo + 13:mo + 14],
                    mst_sb[:, mo + scol:mo + scol + 1]))(),
                    reads=[RMSS if J == 1 else RMSS2], writes=[RMSS2])
                pp.add("vector", (lambda ncols=ncols, PB=PB, mo=mo: lambda e:
                                  e.tensor_scalar_mul(
                    PB[:, 0:ncols], PB[:, 0:ncols],
                    mst_sb[:, mo + 13:mo + 14]))(),
                    reads=[RPB, RMSS2], writes=[RPB])

            def m4_B4a(k):
                m = m4_meta(k)
                t, h, g, ntk, par = m["t"], m["h"], m["g"], m["ntk"], m["par"]
                PB, PTB = PBp[par], PTBp[par]
                RPB, RPTB = f"PB{par}", f"PTB{par}"
                for r in range((ntk + 7) // 8):
                    n = min(8, ntk - r * 8)

                    def ptr(e, r=r, n=n, PB=PB):
                        for b in range(n):
                            c = r * 8 + b
                            ins = e.transpose(pT[:, b * 128:(b + 1) * 128],
                                              PB[:, c * 128:(c + 1) * 128],
                                              ident_sb[:])
                        return ins
                    pp.add("tensor", ptr, reads=[RPB, CONST], writes=["pT"])
                    pp.add("vector", (lambda r=r, n=n, PTB=PTB:
                                      lambda e: e.tensor_copy(
                        PTB[:, r * 1024:r * 1024 + n * 128],
                        pT[:, 0:n * 128]))(), reads=["pT"], writes=[RPTB])

                PO = f"pO{h % 2}"

                def pv(e, g=g, ntk=ntk, h=h, PTB=PTB):
                    for c in range(ntk):
                        ins = e.matmul(pO[h % 2],
                                       v_sb[:, c * 512 + g * 128:
                                            c * 512 + (g + 1) * 128],
                                       PTB[:, c * 128:(c + 1) * 128],
                                       start=(c == 0), stop=(c == ntk - 1))
                    return ins
                pp.add("tensor", pv, reads=[RPTB] + vres, writes=[PO])

            def m4_B4b(k):
                m = m4_meta(k)
                t, h = m["t"], m["h"]
                PO = f"pO{h % 2}"
                pp.add("scalar", (lambda h=h, t=t: lambda e: e.copy(
                    OT_sb[:, h * 512 + t * 128:h * 512 + (t + 1) * 128],
                    pO[h % 2]))(), reads=[PO], writes=["OT"])

            N4 = len(its)
            for k in range(N4):
                m4_F(k)
                if k >= 1:
                    m4_B3(k - 1)
                    m4_B4a(k - 1)
                if k >= 2:
                    m4_B4b(k - 2)
                m4_B1(k)
                m4_B2(k)
            m4_B3(N4 - 1)
            m4_B4a(N4 - 1)
            m4_B4b(N4 - 2)
            m4_B4b(N4 - 1)

            if s == 0 and "OT" in taps:
                t6 = tapt("OT", [128, 8192], BF16)
                pp.add("gpsimd", lambda e: e.dma_start(out=t6, in_=OT_sb),
                       reads=["OT"], writes=["OUT"], group="g_tap")

            if maxphase < 5:
                continue
            for ds in range(DS):
                for dqc in range(HQ):
                    if dqc % 4 == 0:
                        wr = next_wst()
                        pp.add("sync", (lambda ds=ds, dqc=dqc, wr=wr:
                                        lambda e: e.dma_start(
                            out=wst_sb[wr][:].rearrange("p (b c) -> p b c", c=512),
                            in_=woT_s[(ds * 16 + dqc) * 128:
                                      (ds * 16 + dqc + 4) * 128, :].rearrange(
                                "(b p) c -> p b c", p=128)))(),
                            reads=["woTs"], writes=[f"wst{wr}"],
                            group=f"g_wst{wr}")
                        cwr = wr

                    def oproj(e, dqc=dqc, wr=cwr):
                        for t in range(TQ):
                            ins = e.matmul(
                                p_out[:, t * 512:(t + 1) * 512],
                                OT_sb[:, dqc * 512 + t * 128:
                                      dqc * 512 + (t + 1) * 128],
                                wst_sb[wr][:, (dqc % 4) * 512:(dqc % 4 + 1) * 512],
                                start=(dqc == 0), stop=(dqc == HQ - 1))
                        return ins
                    pp.add("tensor", oproj, reads=["OT", f"wst{cwr}"],
                           writes=["pO0", "pO1", "pO2", "pO3"])
                for t in range(TQ):
                    pp.add("vector", (lambda ds=ds, t=t: lambda e:
                                      e.scalar_tensor_tensor(
                        xs_sb[:, t * 2048 + ds * 512:t * 2048 + (ds + 1) * 512],
                        p_out[:, t * 512:(t + 1) * 512], 1.0,
                        xs_sb[:, t * 2048 + ds * 512:t * 2048 + (ds + 1) * 512],
                        ALU.mult, ALU.add))(),
                        reads=[f"pO{t}", f"xs{t}"], writes=[f"xs{t}"])

            if s == 0 and "tmp" in taps:
                t7 = tapt("tmp", [128, 8192], F32)
                pp.add("gpsimd", lambda e: e.dma_start(out=t7, in_=xs_sb[:]),
                       reads=["xs0", "xs1", "xs2", "xs3"], writes=["OUT"],
                       group="g_tap")

            if maxphase < 6:
                continue
            for hc in range(GH):
                par = hc % 2
                wr1 = next_wst()
                pp.add("sync", (lambda hc=hc, wr1=wr1: lambda e: e.dma_start(
                    out=wst_sb[wr1][:],
                    in_=wguT_s[hc * 128:(hc + 1) * 128, :]))(),
                    reads=["wguTs"], writes=[f"wst{wr1}"], group=f"g_wst{wr1}")
                wr2 = next_wst()
                pp.add("sync", (lambda hc=hc, wr2=wr2: lambda e: e.dma_start(
                    out=wst_sb[wr2][:],
                    in_=wguT_s[(64 + hc) * 128:(65 + hc) * 128, :]))(),
                    reads=["wguTs"], writes=[f"wst{wr2}"], group=f"g_wst{wr2}")

                def gmm(e, wr1=wr1):
                    for dc in range(DC):
                        ins = e.matmul(pA[:],
                                       wst_sb[wr1][:, dc * 128:(dc + 1) * 128],
                                       xnT_sb[:, dc * 512:(dc + 1) * 512],
                                       start=(dc == 0), stop=(dc == DC - 1))
                    return ins
                pp.add("tensor", gmm, reads=[f"wst{wr1}", "xnT"], writes=["pA"])
                pp.add("scalar", (lambda par=par: lambda e: e.activation(
                    As_sb[0][:], pA[:], AF.Silu))(),
                    reads=["pA"], writes=["As0"])

                def umm(e, wr2=wr2):
                    for dc in range(DC):
                        ins = e.matmul(pB[:],
                                       wst_sb[wr2][:, dc * 128:(dc + 1) * 128],
                                       xnT_sb[:, dc * 512:(dc + 1) * 512],
                                       start=(dc == 0), stop=(dc == DC - 1))
                    return ins
                pp.add("tensor", umm, reads=[f"wst{wr2}", "xnT"], writes=["pB"])
                pp.add("vector", (lambda hc=hc, par=par: lambda e: e.tensor_tensor(
                    big_sb[:, hc * 512:(hc + 1) * 512], As_sb[0][:], pB[:],
                    ALU.mult))(),
                    reads=["As0", "pB"], writes=["H"])
                pp.add("scalar", (lambda hc=hc, par=par: lambda e: e.activation(
                    hsq_sb[0][:], big_sb[:, hc * 512:(hc + 1) * 512],
                    AF.Square))(),
                    reads=["H"], writes=["hsq0"])
                pp.add("tensor", (lambda hc=hc, par=par: lambda e: e.matmul(
                    pC[:], ones_sb[:], hsq_sb[0][:],
                    start=(hc == 0), stop=(hc == GH - 1)))(),
                    reads=["hsq0", CONST], writes=["pC"])
            pp.add("scalar", lambda e: e.activation(
                rbb_sb[:], pC[:], AF.Sqrt, bias=eps_ff[:], scale=1.0 / 8192.0),
                reads=["pC", CONST], writes=["rbb"])
            pp.add("vector", lambda e: e.reciprocal(rkb_sb[:], rbb_sb[:]),
                   reads=["rbb"], writes=["rkb"])

            def invtr(e):
                for t in range(TQ):
                    ins = e.transpose(pA[:, t * 128:(t + 1) * 128],
                                      rkb_sb[:, t * 128:(t + 1) * 128],
                                      identf_sb[:])
                return ins
            pp.add("tensor", invtr, reads=["rkb", CONST], writes=["pA"])

            def invcol(e):
                for t in range(TQ):
                    ins = e.tensor_scalar_mul(invc_sb[:, t:t + 1],
                                              pA[:, t * 128:t * 128 + 1], 1.0)
                return ins
            pp.add("vector", invcol, reads=["pA"], writes=["invc"])

            if s == 0 and "H" in taps:
                t8 = tapt("H", [128, 1024], BF16)
                pp.add("gpsimd", lambda e: e.dma_start(out=t8,
                                                       in_=big_sb[:, 0:1024]),
                       reads=["H"], writes=["OUT"], group="g_tap")
            if s == 0 and "invc" in taps:
                t9 = tapt("invc", [128, 4], F32)
                pp.add("gpsimd", lambda e: e.dma_start(out=t9, in_=invc_sb[:]),
                       reads=["invc"], writes=["OUT"], group="g_tap")

            if maxphase < 7:
                continue
            for ds in range(DS):
                for hc in range(GH):
                    if hc % 4 == 0:
                        wr = next_wst()
                        pp.add("sync", (lambda ds=ds, hc=hc, wr=wr:
                                        lambda e: e.dma_start(
                            out=wst_sb[wr][:].rearrange("p (b c) -> p b c", c=512),
                            in_=wpT_s[(ds * 64 + hc) * 128:
                                      (ds * 64 + hc + 4) * 128, :].rearrange(
                                "(b p) c -> p b c", p=128)))(),
                            reads=["wpTs"], writes=[f"wst{wr}"],
                            group=f"g_wst{wr}")
                        cwr = wr

                    def fproj(e, hc=hc, wr=cwr):
                        for t in range(TQ):
                            ins = e.matmul(
                                p_out[:, t * 512:(t + 1) * 512],
                                big_sb[:, hc * 512 + t * 128:
                                       hc * 512 + (t + 1) * 128],
                                wst_sb[wr][:, (hc % 4) * 512:(hc % 4 + 1) * 512],
                                start=(hc == 0), stop=(hc == GH - 1))
                        return ins
                    pp.add("tensor", fproj, reads=["H", f"wst{cwr}"],
                           writes=["pO0", "pO1", "pO2", "pO3"])
                for t in range(TQ):
                    ob = 0
                    pp.add("vector", (lambda ds=ds, t=t, ob=ob: lambda e:
                                      e.scalar_tensor_tensor(
                        outb_sb[ob][:],
                        p_out[:, t * 512:(t + 1) * 512], invc_sb[:, t:t + 1],
                        xs_sb[:, t * 2048 + ds * 512:t * 2048 + (ds + 1) * 512],
                        ALU.mult, ALU.add))(),
                        reads=[f"pO{t}", "invc", f"xs{t}"], writes=[f"outb{ob}"])
                    pp.add("gpsimd", (lambda s=s, ds=ds, t=t, ob=ob:
                                      lambda e: e.dma_start(
                        out=out_d[s * TS + t * 128:s * TS + (t + 1) * 128,
                                  ds * 512:(ds + 1) * 512],
                        in_=outb_sb[ob][:]))(),
                        reads=[f"outb{ob}"], writes=["OUT"],
                        group=f"g_out{ob}")

        pp.add("vector", lambda e: e.engine_nop(), reads=["OUT"])
        pp.emit_block()

    return nc, dbg


_CACHE = {}


def kernel(L0_attn_x, L0_attn_wq, L0_attn_wk, L0_attn_wv, L0_attn_wo,
           L0_attn_q_gamma, L0_attn_k_gamma, L0_ffn_w_gate_up, L0_ffn_w_proj):
    if "nc" not in _CACHE:
        _CACHE["nc"] = build()[0]
    nc = _CACHE["nc"]
    x = np.asarray(L0_attn_x, dtype=np.float32)
    shared = {
        "wq": np.ascontiguousarray(np.asarray(L0_attn_wq, np.float32)),
        "wk": np.ascontiguousarray(np.asarray(L0_attn_wk, np.float32)),
        "wv": np.ascontiguousarray(np.asarray(L0_attn_wv, np.float32)),
        "wo": np.ascontiguousarray(np.asarray(L0_attn_wo, np.float32)),
        "qg": np.ascontiguousarray(np.asarray(L0_attn_q_gamma, np.float32)),
        "kg": np.ascontiguousarray(np.asarray(L0_attn_k_gamma, np.float32)),
        "wgu": np.ascontiguousarray(np.asarray(L0_ffn_w_gate_up, np.float32)),
        "wp": np.ascontiguousarray(np.asarray(L0_ffn_w_proj, np.float32)),
    }
    in_maps = []
    for b in range(8):
        m = dict(shared)
        m["x"] = np.ascontiguousarray(x[b])
        in_maps.append(m)
    res = run_bass_kernel_spmd(nc, in_maps, core_ids=list(range(8)))
    return np.stack([np.asarray(res.results[b]["out"], np.float32)
                     for b in range(8)], axis=0)


# version 16 (final)
# speedup vs baseline: 1.0528x; 1.0528x over previous
import numpy as np
import ml_dtypes
from contextlib import ExitStack

import concourse.bass as bass
import concourse.mybir as mybir
from concourse.bass_utils import run_bass_kernel_spmd

F32 = mybir.dt.float32
BF16 = mybir.dt.bfloat16
AF = mybir.ActivationFunctionType
ALU = mybir.AluOpType
AX = mybir.AxisListType
BF = ml_dtypes.bfloat16

TT = 2048
DD = 2048
DC = 16
HQ = 16
KV = 4
HD = 128
SL = 4
TS = 512
TQ = 4
GH = 64
DS = 4
CAP = 50.0
SCALE = float(HD) ** -0.5
THC = 0.5009765625
EPS_QK = 1e-6
EPS_LN = 1e-5
EPS_FF = float(np.finfo(np.float32).eps)

ENGS = ("sync", "gpsimd", "tensor", "scalar", "vector")


class Pipe:

    COMPUTE = ("tensor", "scalar", "vector")

    def __init__(self, nc, ctx):
        self.nc = nc
        self.ctx = ctx
        self.sems = {}
        self.pos = {}
        self.stream = {e: [] for e in ENGS}
        self.writers = {}
        self.readers = {}

    def _sem(self, name):
        if name not in self.sems:
            self.sems[name] = self.ctx.enter_context(
                self.nc.semaphore("sem_" + name))
            self.pos[name] = 0
        return self.sems[name]

    ALIAS = {"H": ("qT", "TB0", "PB0", "PTB0", "TB1", "PB1", "PTB1", "OT"),
             "qT": ("H",), "TB0": ("H",), "PB0": ("H",), "PTB0": ("H",),
             "TB1": ("H",), "PB1": ("H",), "PTB1": ("H",), "OT": ("H",)}
    PSUMS = ("pA", "pB", "pC", "pT", "pO0", "pO1", "pO2", "pO3",
             "pW0", "pW1")

    def add(self, eng, fn, reads=(), writes=(), group=None):
        is_dma = eng in ("sync", "gpsimd")
        grp = group if group is not None else eng
        assert not (is_dma and grp == eng), \
            f"DMA step on {eng} needs an explicit slot/aggregate group"
        inc = 16 if is_dma else 1
        self._sem(grp)
        waits = {}

        def dep(d, kind):
            for sm, (pe, pv) in d.items():
                if pe == eng:
                    if kind == "raw":
                        if eng == "tensor":
                            continue
                    else:
                        if eng in self.COMPUTE:
                            continue
                waits[sm] = max(waits.get(sm, 0), pv)

        for r in reads:
            for rr in (r,) + self.ALIAS.get(r, ()):
                dep(self.writers.get(rr, {}), "raw")
            if r in self.PSUMS:
                dep(self.readers.get(r, {}), "war")
        for r in writes:
            for rr in (r,) + self.ALIAS.get(r, ()):
                dep(self.readers.get(rr, {}), "war")
                dep(self.writers.get(rr, {}), "waw")

        self.pos[grp] += inc
        after = self.pos[grp]
        for r in reads:
            self.readers.setdefault(r, {})[grp] = (eng, after)
        for r in writes:
            self.writers.setdefault(r, {})[grp] = (eng, after)
        self.stream[eng].append((fn, waits, grp))
        return after

    def drain_dmas(self):
        waits = {sm: self.pos[sm] for sm in self.sems
                 if sm not in ENGS and self.pos[sm] > 0}
        self.pos["vector"] = self.pos.get("vector", 0) + 1
        self._sem("vector")
        self.stream["vector"].append(
            (lambda e: e.engine_nop(), waits, "vector"))

    def emit_block(self):
        nc = self.nc
        sems = self.sems
        streams = {e: list(self.stream[e]) for e in ENGS}
        for e in ENGS:
            self.stream[e] = []
        with nc.Block() as block:
            def mk(name):
                def inner(e):
                    inc = 16 if name in ("sync", "gpsimd") else 1
                    watermark = {}
                    for fn, waits, grp in streams[name]:
                        for sm, pv in sorted(waits.items()):
                            if watermark.get(sm, 0) >= pv:
                                continue
                            e.wait_ge(sems[sm], pv)
                            watermark[sm] = pv
                        fn(e).then_inc(sems[grp], inc)
                return inner
            block.sync(mk("sync"))
            block.gpsimd(mk("gpsimd"))
            block.tensor(mk("tensor"))
            block.scalar(mk("scalar"))
            block.vector(mk("vector"))


def sap(handle, pitch, col0, nblk, stride, width=128):
    return bass.AP(handle, col0, [[pitch, 128], [stride, nblk], [1, width]])


def build(taps=(), nslices=SL, maxphase=7, m3sub=99):
    taps = set(taps)
    nc = bass.Bass("TRN2", target_bir_lowering=False)

    x_d = nc.dram_tensor("x", [TT, DD], F32, kind="ExternalInput").ap()
    wq_d = nc.dram_tensor("wq", [DD, DD], F32, kind="ExternalInput").ap()
    wk_d = nc.dram_tensor("wk", [512, DD], F32, kind="ExternalInput").ap()
    wv_d = nc.dram_tensor("wv", [512, DD], F32, kind="ExternalInput").ap()
    wo_d = nc.dram_tensor("wo", [DD, DD], F32, kind="ExternalInput").ap()
    qg_d = nc.dram_tensor("qg", [HD], F32, kind="ExternalInput").ap()
    kg_d = nc.dram_tensor("kg", [HD], F32, kind="ExternalInput").ap()
    wgu_d = nc.dram_tensor("wgu", [16384, DD], F32, kind="ExternalInput").ap()
    wp_d = nc.dram_tensor("wp", [DD, 8192], F32, kind="ExternalInput").ap()
    out_d = nc.dram_tensor("out", [TT, DD], F32, kind="ExternalOutput").ap()

    wqT_s = nc.dram_tensor("wqT_s", [16 * 128, 2048], BF16, kind="Internal").ap()
    wguT_s = nc.dram_tensor("wguT_s", [128 * 128, 2048], BF16, kind="Internal").ap()
    woT_s = nc.dram_tensor("woT_s", [4 * 16 * 128, 512], BF16, kind="Internal").ap()
    wpT_s = nc.dram_tensor("wpT_s", [4 * 64 * 128, 512], BF16, kind="Internal").ap()

    ident_np = np.eye(128, dtype=BF)
    identf_np = np.eye(128, dtype=np.float32)
    ones_np = np.ones((128, 128), dtype=BF)
    tri_np = np.where(np.arange(128)[None, :] <= np.arange(128)[:, None],
                      np.float32(9.0), np.float32(-1e5)).astype(np.float32)
    ident_d = nc.inline_tensor(ident_np, name="identc").ap()
    identf_d = nc.inline_tensor(identf_np, name="identfc").ap()
    ones_d = nc.inline_tensor(ones_np, name="onesc").ap()
    tri_d = nc.inline_tensor(tri_np, name="tric").ap()

    dbg = {}

    def tapt(name, shape, dt):
        ap = nc.dram_tensor("tap_" + name, list(shape), dt,
                            kind="ExternalOutput").ap()
        dbg[name] = ap
        return ap

    with ExitStack() as ctx:
        pp = Pipe(nc, ctx)

        sb = lambda n_, s_, d_: ctx.enter_context(nc.sbuf_tensor(n_, s_, d_))
        ident_sb = sb("ident_sb", [128, 128], BF16)
        identf_sb = sb("identf_sb", [128, 128], F32)
        ones_sb = sb("ones_sb", [128, 128], BF16)
        tri_sb = sb("tri_sb", [128, 128], F32)
        gq_sb = sb("gq_sb", [128, 1], F32)
        gk_sb = sb("gk_sb", [128, 1], F32)
        eps_ln = sb("eps_ln", [128, 1], F32)
        eps_qk = sb("eps_qk", [128, 1], F32)
        eps_ff = sb("eps_ff", [128, 1], F32)
        wkT_sb = sb("wkT_sb", [128, 8192], BF16)
        wvT_sb = sb("wvT_sb", [128, 8192], BF16)

        CONST = "consts"
        pp.add("sync", lambda e: e.dma_start(out=ident_sb[:], in_=ident_d),
               writes=[CONST], group="g_const")
        pp.add("sync", lambda e: e.dma_start(out=identf_sb[:], in_=identf_d),
               writes=[CONST], group="g_const")
        pp.add("sync", lambda e: e.dma_start(out=ones_sb[:], in_=ones_d),
               writes=[CONST], group="g_const")
        pp.add("sync", lambda e: e.dma_start(out=tri_sb[:], in_=tri_d),
               writes=[CONST], group="g_const")
        pp.add("sync", lambda e: e.dma_start(
            out=gq_sb[:], in_=qg_d.rearrange("(p o) -> p o", o=1)), writes=[CONST], group="g_const")
        pp.add("sync", lambda e: e.dma_start(
            out=gk_sb[:], in_=kg_d.rearrange("(p o) -> p o", o=1)), writes=[CONST], group="g_const")
        pp.add("vector", lambda e: e.memset(eps_ln[:], EPS_LN), writes=[CONST])
        pp.add("vector", lambda e: e.memset(eps_qk[:], EPS_QK), writes=[CONST])
        pp.add("vector", lambda e: e.memset(eps_ff[:], EPS_FF), writes=[CONST])

        with ExitStack() as wctx:
            wsb = lambda n_, s_, d_: wctx.enter_context(nc.sbuf_tensor(n_, s_, d_))
            NWS = 3
            wnat = [wsb(f"wnat{i}", [128, 2048], BF16) for i in range(NWS)]
            what = [wsb(f"what{i}", [128, 2048], BF16) for i in range(NWS)]
            u_t = wsb("u_t", [128, 2048], BF16)
            sg_t = [wsb(f"sg_t{i}", [128, 2048], BF16) for i in range(NWS)]
            aw_t = [wsb(f"aw_t{i}", [128, 2048], BF16) for i in range(NWS)]
            gsum = wsb("gsum", [128, 128], F32)
            sc_bf = wsb("sc_bf", [128, 128], BF16)
            th_f = wsb("th_f", [128, 128], F32)
            pstage = [wsb(f"pstage{i}", [128, 2048], BF16) for i in range(NWS)]
            dstage = wsb("dstage", [128, 16 * 512], BF16)
            pW = [wctx.enter_context(nc.psum_tensor(f"pW{i}", [128, 1024], BF16))
                  for i in range(2)]
            rctr = [0]

            def quant_a(wcols, sl):
                WN = f"wnat{sl}"
                pp.add("scalar", (lambda sl=sl, wcols=wcols: lambda e: e.activation(
                    aw_t[sl][:, 0:wcols], wnat[sl][:, 0:wcols], AF.Abs))(),
                    reads=[WN], writes=[f"awt{sl}"])
                pp.add("scalar", (lambda sl=sl, wcols=wcols: lambda e: e.activation(
                    sg_t[sl][:, 0:wcols], wnat[sl][:, 0:wcols], AF.Sign))(),
                    reads=[WN], writes=[f"sgt{sl}"])

            def quant_steps(wcols, sl):
                G = wcols // 64
                WN = f"wnat{sl}"
                WH = f"what{sl}"
                w3 = wnat[sl][:, 0:wcols].rearrange("p (g k) -> p g k", k=64)
                thb = th_f[:, 0:G].unsqueeze(2).broadcast_to([128, G, 64])
                scb = sc_bf[:, 0:G].unsqueeze(2).broadcast_to([128, G, 64])
                u3 = u_t[:, 0:wcols].rearrange("p (g k) -> p g k", k=64)
                s3 = sg_t[sl][:, 0:wcols].rearrange("p (g k) -> p g k", k=64)
                a3 = aw_t[sl][:, 0:wcols].rearrange("p (g k) -> p g k", k=64)
                h3 = what[sl][:, 0:wcols].rearrange("p (g k) -> p g k", k=64)
                pp.add("vector", lambda e: e.tensor_reduce(
                    gsum[:, 0:G], w3, AX.X, ALU.add, apply_absolute_value=True),
                    reads=[WN], writes=["gsum"])
                pp.add("vector", lambda e: e.tensor_scalar(
                    sc_bf[:, 0:G], gsum[:, 0:G], 1.0 / 64.0, 1e-8,
                    ALU.mult, ALU.max),
                    reads=["gsum"], writes=["scbf"])
                pp.add("vector", lambda e: e.tensor_scalar_mul(
                    th_f[:, 0:G], sc_bf[:, 0:G], THC),
                    reads=["scbf"], writes=["thf"])
                pp.add("vector", (lambda sl=sl: lambda e: e.tensor_tensor(
                    u3, a3, thb, ALU.is_gt))(),
                    reads=[f"awt{sl}", "thf"], writes=["ut"])
                pp.add("vector", (lambda sl=sl: lambda e: e.tensor_tensor(
                    u3, u3, s3, ALU.mult))(),
                    reads=[f"ut", f"sgt{sl}"], writes=["ut"])
                pp.add("vector", lambda e: e.tensor_tensor(h3, u3, scb, ALU.mult),
                       reads=["ut", "scbf"], writes=[WH])

            def wclass(src, ntiles, wcols, quant, mode, dst=None, dstres=None,
                       dst_sb=None, dst_pitch=None, colmap=None, sbres=None,
                       srcfn=None, wrowfn=None):
                nb = wcols // 128

                def rest(i, sl):
                    WN = f"wnat{sl}"
                    WH = f"what{sl}"
                    PS = f"pstage{sl}"
                    if quant:
                        quant_steps(wcols, sl)
                    srcT = what[sl] if quant else wnat[sl]
                    SR = WH if quant else WN
                    for r in range((nb + 7) // 8):
                        n = min(8, nb - r * 8)

                        pwp = rctr[0] % 2
                        rctr[0] += 1

                        def trn(e, r=r, n=n, srcT=srcT, pwp=pwp):
                            for b in range(n):
                                blk = r * 8 + b
                                ins = e.transpose(
                                    pW[pwp][:, b * 128:(b + 1) * 128],
                                    srcT[:, blk * 128:(blk + 1) * 128],
                                    ident_sb[:])
                            return ins
                        pp.add("tensor", trn, reads=[SR, CONST],
                               writes=[f"pW{pwp}"])

                        if mode == "panel":
                            dfn = (lambda r=r, n=n, sl=sl: lambda: (
                                pstage[sl][:, r * 1024:r * 1024 + n * 128]))()
                            wres = PS
                        elif mode == "sbuf":
                            dfn = (lambda i=i, r=r, n=n: lambda: sap(
                                dst_sb, dst_pitch, colmap(i, r), n,
                                colmap("stride", 0)))()
                            wres = sbres
                        else:
                            dfn = (lambda i=i, r=r, n=n: lambda: sap(
                                dstage, 16 * 512, (r * 8) * 512 + (i % 4) * 128,
                                n, 512))()
                            wres = f"dstage{r}"
                        pp.add("scalar", (lambda dfn=dfn, n=n, pwp=pwp:
                                          lambda e: e.copy(
                            dfn(), pW[pwp][:, 0:n * 128]))(),
                            reads=[f"pW{pwp}"], writes=[wres])
                        if mode == "dsgroup" and i % 4 == 3:
                            for hb in range(r * 8, r * 8 + n):
                                row = (wrowfn(i, hb) if wrowfn
                                       else (i // 4) * nb + hb)
                                pp.add("sync", (lambda row=row, hb=hb:
                                                lambda e: e.dma_start(
                                    out=dst[row * 128:(row + 1) * 128, :],
                                    in_=dstage[:, hb * 512:(hb + 1) * 512]))(),
                                    reads=[f"dstage{r}"], writes=[dstres],
                                    group="g_" + dstres)
                    if mode == "panel":
                        pp.add("sync", (lambda i=i, sl=sl, wcols=wcols:
                                        lambda e: e.dma_start(
                            out=dst[i * 128:(i + 1) * 128, :],
                            in_=pstage[sl][:, 0:wcols]))(),
                            reads=[PS], writes=[dstres], group=f"g_ws{sl}")

                pending = [None]
                for i in list(range(ntiles)) + [None]:
                    if i is not None:
                        sl = i % NWS
                        WN = f"wnat{sl}"
                        if srcfn is None:
                            sfn = (lambda i=i: lambda:
                                   src[i * 128:(i + 1) * 128, :])()
                        else:
                            sfn = (lambda i=i: lambda: srcfn(i))()
                        pp.add("gpsimd", (lambda sfn=sfn, sl=sl, wcols=wcols:
                                          lambda e: e.dma_start(
                            out=wnat[sl][:, 0:wcols], in_=sfn()))(), writes=[WN],
                            group=f"g_{WN}")
                        if quant:
                            quant_a(wcols, sl)
                    if pending[0] is not None:
                        pending[0]()
                        pending[0] = None
                    if i is not None:
                        pending[0] = (lambda i=i, sl=sl: lambda: rest(i, sl))()

            def kmap(i, r=None):
                if i == "stride":
                    return 128
                return i * 2048 + (r * 8) * 128

            def vmap(i, r=None):
                if i == "stride":
                    return 512
                return (r * 8) * 512 + i * 128

            wclass(wk_d, 4, 2048, False, "sbuf", dst_sb=wkT_sb, dst_pitch=8192,
                   colmap=kmap, sbres="wkTsb")
            wclass(wv_d, 4, 2048, False, "sbuf", dst_sb=wvT_sb, dst_pitch=8192,
                   colmap=vmap, sbres="wvTsb")
            wclass(wq_d, 16, 2048, False, "panel", dst=wqT_s, dstres="wqTs")
            wclass(wgu_d, 128, 2048, True, "panel", dst=wguT_s, dstres="wguTs")
            wclass(wo_d, 16, 2048, False, "dsgroup", dst=woT_s, dstres="woTs")
            def wp_src(i):
                g, cc, ii = i // 16, (i // 4) % 4, i % 4
                row = (g * 4 + ii) * 128
                return wp_d[row:row + 128, cc * 2048:(cc + 1) * 2048]

            def wp_row(i, hb):
                g, cc = i // 16, (i // 4) % 4
                return g * 64 + cc * 16 + hb
            wclass(None, 64, 2048, True, "dsgroup", dst=wpT_s, dstres="wpTs",
                   srcfn=wp_src, wrowfn=wp_row)

            for nm, src_, shp, res in [
                    ("wguT0", lambda: wguT_s[0:128, :], [128, 2048], "wguTs"),
                    ("wkT", lambda: wkT_sb[:], [128, 8192], "wkTsb"),
                    ("wvT", lambda: wvT_sb[:], [128, 8192], "wvTsb"),
                    ("wpT0", lambda: wpT_s[0:128, :], [128, 512], "wpTs"),
                    ("woT0", lambda: woT_s[0:128, :], [128, 512], "woTs"),
                    ("wqT0", lambda: wqT_s[0:128, :], [128, 2048], "wqTs")]:
                if nm in taps:
                    tdst = tapt(nm, shp, BF16)
                    pp.add("gpsimd", (lambda tdst=tdst, src_=src_:
                                      lambda e: e.dma_start(
                                          out=tdst, in_=src_()))(),
                           reads=[res], writes=["OUT"], group="g_tap")

            pp.drain_dmas()
            pp.emit_block()

        xs_sb = sb("xs_sb", [128, 4 * 2048], F32)
        xnT_sb = sb("xnT_sb", [128, DC * 512], BF16)
        knT_sb = sb("knT_sb", [128, KV * 2048], BF16)
        v_sb = sb("v_sb", [128, 16 * 512], BF16)
        big_sb = sb("big_sb", [128, 32768], BF16)
        sqs_sb = sb("sqs_sb", [128, 512], BF16)
        rbb_sb = sb("rbb_sb", [128, 512], F32)
        rkb_sb = sb("rkb_sb", [128, 512], F32)
        rqc_sb = sb("rqc_sb", [128, 64], F32)
        mst_sb = sb("mst_sb", [128, 32], F32)
        stat_sb = sb("stat_sb", [128, 8], F32)
        invc_sb = sb("invc_sb", [128, 4], F32)
        As_sb = [sb("As0", [128, 512], F32)]
        hsq_sb = [sb("hsq0", [128, 512], BF16)]
        outb_sb = [sb("outb0", [128, 512], F32)]
        NWST = 4
        wst_sb = [sb(f"wst{i}", [128, 2048], BF16) for i in range(NWST)]
        wring = [0]

        def next_wst():
            r_ = wring[0] % NWST
            wring[0] += 1
            return r_

        qT_sb = big_sb[:, 0:8192]
        OT_sb = big_sb[:, 8192:16384]
        TBp = [big_sb[:, 16384:20480].bitcast(F32),
               big_sb[:, 24576:28672].bitcast(F32)]
        PBp = [big_sb[:, 20480:22528], big_sb[:, 28672:30720]]
        PTBp = [big_sb[:, 22528:24576], big_sb[:, 30720:32768]]

        p_out = ctx.enter_context(nc.psum_tensor("p_out", [128, 2048], F32))
        pA = ctx.enter_context(nc.psum_tensor("pA", [128, 512], F32))
        pB = ctx.enter_context(nc.psum_tensor("pB", [128, 512], F32))
        pC = ctx.enter_context(nc.psum_tensor("pC", [128, 512], F32))
        pT = ctx.enter_context(nc.psum_tensor("pT", [128, 1024], BF16))
        pS = [pA, pB]
        pO = [p_out[:, 0:128], p_out[:, 512:640]]

        for s in range(nslices):
            tok0 = s * TS
            knres = [f"knT{ss}" for ss in range(s + 1)]
            vres = [f"v{ss}" for ss in range(s + 1)]

            for t in range(TQ):
                XS = f"xs{t}"
                xsl = xs_sb[:, t * 2048:(t + 1) * 2048]
                pp.add("gpsimd", (lambda t=t, tok0=tok0: lambda e: e.dma_start(
                    out=xs_sb[:, t * 2048:(t + 1) * 2048],
                    in_=x_d[tok0 + t * 128:tok0 + (t + 1) * 128, :]))(),
                    writes=[XS], group=f"g_xs{t}")
                par = t % 2
                pp.add("scalar", (lambda xsl=xsl, par=par: lambda e: e.activation(
                    TBp[par][:, :], xsl, AF.Square, accum_out=stat_sb[:, 1:2]))(),
                    reads=[XS], writes=[f"TB{par}", "stat"])
                pp.add("vector", (lambda xsl=xsl: lambda e: e.reduce_sum(
                    stat_sb[:, 0:1], xsl, AX.X))(), reads=[XS], writes=["stat"])

                def stats(e):
                    e.tensor_scalar_mul(stat_sb[:, 2:3], stat_sb[:, 0:1],
                                        1.0 / 2048.0)
                    return e.tensor_scalar_mul(stat_sb[:, 5:6], stat_sb[:, 1:2],
                                               1.0 / 2048.0)
                pp.add("vector", stats, reads=["stat"], writes=["stat"])
                pp.add("vector", lambda e: e.tensor_tensor(
                    stat_sb[:, 6:7], stat_sb[:, 2:3], stat_sb[:, 2:3], ALU.mult),
                    reads=["stat"], writes=["stat"])
                pp.add("vector", lambda e: e.tensor_tensor(
                    stat_sb[:, 3:4], stat_sb[:, 5:6], stat_sb[:, 6:7],
                    ALU.subtract), reads=["stat"], writes=["stat"])
                pp.add("scalar", lambda e: e.activation(
                    stat_sb[:, 4:5], stat_sb[:, 3:4], AF.Sqrt, bias=eps_ln[:]),
                    reads=["stat", CONST], writes=["stat"])
                pp.add("vector", lambda e: e.reciprocal(stat_sb[:, 7:8],
                                                        stat_sb[:, 4:5]),
                       reads=["stat"], writes=["stat"])
                pp.add("vector", (lambda xsl=xsl, par=par: lambda e: e.tensor_scalar(
                    TBp[par].bitcast(BF16)[:, 0:2048], xsl, stat_sb[:, 2:3],
                    stat_sb[:, 7:8], ALU.subtract, ALU.mult))(),
                    reads=[XS, "stat"], writes=[f"TB{par}"])
                for r in range(2):
                    def trn(e, r=r, par=par):
                        xnb = TBp[par].bitcast(BF16)
                        for b in range(8):
                            ins = e.transpose(pT[:, b * 128:(b + 1) * 128],
                                              xnb[:, (r * 8 + b) * 128:
                                                  (r * 8 + b + 1) * 128],
                                              ident_sb[:])
                        return ins
                    pp.add("tensor", trn, reads=[f"TB{par}", CONST], writes=["pT"])
                    pp.add("scalar", (lambda t=t, r=r: lambda e: e.copy(
                        sap(xnT_sb, DC * 512, (r * 8) * 512 + t * 128, 8, 512),
                        pT[:, 0:1024]))(), reads=["pT"], writes=["xnT"])

            if maxphase < 2:
                continue
            for kvc in range(KV):
                def kproj(e, kvc=kvc):
                    for dc in range(DC):
                        ins = e.matmul(pA[:],
                                       wkT_sb[:, kvc * 2048 + dc * 128:
                                              kvc * 2048 + (dc + 1) * 128],
                                       xnT_sb[:, dc * 512:(dc + 1) * 512],
                                       start=(dc == 0), stop=(dc == DC - 1))
                    return ins
                pp.add("tensor", kproj, reads=["wkTsb", "xnT"], writes=["pA"])
                pp.add("scalar", lambda e: e.activation(sqs_sb[:], pA[:],
                                                        AF.Square),
                       reads=["pA"], writes=["sqs"])
                pp.add("tensor", lambda e: e.matmul(pB[:], ones_sb[:], sqs_sb[:],
                                                    start=True, stop=True),
                       reads=["sqs", CONST], writes=["pB"])
                pp.add("scalar", lambda e: e.activation(
                    rbb_sb[:], pB[:], AF.Sqrt, bias=eps_qk[:], scale=1.0 / 128.0),
                    reads=["pB", CONST], writes=["rbb"])
                pp.add("vector", lambda e: e.reciprocal(rkb_sb[:], rbb_sb[:]),
                       reads=["rbb"], writes=["rkb"])
                pp.add("vector", (lambda kvc=kvc, tok0=tok0: lambda e:
                                  e.scalar_tensor_tensor(
                    knT_sb[:, kvc * 2048 + tok0:kvc * 2048 + tok0 + TS],
                    pA[:], gk_sb[:], rkb_sb[:], ALU.mult, ALU.mult))(),
                    reads=["pA", "rkb", CONST], writes=[f"knT{s}"])
            for t in range(TQ):
                def vproj(e, t=t):
                    for dc in range(DC):
                        ins = e.matmul(pC[:],
                                       xnT_sb[:, dc * 512 + t * 128:
                                              dc * 512 + (t + 1) * 128],
                                       wvT_sb[:, dc * 512:(dc + 1) * 512],
                                       start=(dc == 0), stop=(dc == DC - 1))
                    return ins
                pp.add("tensor", vproj, reads=["xnT", "wvTsb"], writes=["pC"])
                pp.add("vector", (lambda s=s, t=t: lambda e: e.tensor_copy(
                    v_sb[:, (s * TQ + t) * 512:(s * TQ + t + 1) * 512], pC[:]))(),
                    reads=["pC"], writes=[f"v{s}"])

            if maxphase < 3:
                continue
            for hq in range(HQ):
                wr = next_wst()
                WR = f"wst{wr}"
                pp.add("sync", (lambda hq=hq, wr=wr: lambda e: e.dma_start(
                    out=wst_sb[wr][:], in_=wqT_s[hq * 128:(hq + 1) * 128, :]))(),
                    reads=["wqTs"], writes=[WR], group=f"g_wst{wr}")

                if m3sub < 1:
                    continue
                def qproj(e, wr=wr):
                    for dc in range(DC):
                        ins = e.matmul(pA[:],
                                       wst_sb[wr][:, dc * 128:(dc + 1) * 128],
                                       xnT_sb[:, dc * 512:(dc + 1) * 512],
                                       start=(dc == 0), stop=(dc == DC - 1))
                    return ins
                pp.add("tensor", qproj, reads=[WR, "xnT"], writes=["pA"])
                if m3sub < 2:
                    continue
                pp.add("vector", (lambda hq=hq: lambda e: e.tensor_scalar_mul(
                    qT_sb[:, hq * 512:(hq + 1) * 512], pA[:], gq_sb[:]))(),
                    reads=["pA", CONST], writes=["qT"])
                if m3sub < 3:
                    continue
                pp.add("scalar", lambda e: e.activation(sqs_sb[:], pA[:],
                                                        AF.Square),
                       reads=["pA"], writes=["sqs"])
                if m3sub < 4:
                    continue
                pp.add("tensor", lambda e: e.matmul(pB[:], ones_sb[:], sqs_sb[:],
                                                    start=True, stop=True),
                       reads=["sqs", CONST], writes=["pB"])
                if m3sub < 5:
                    continue
                pp.add("scalar", lambda e: e.activation(
                    rbb_sb[:], pB[:], AF.Sqrt, bias=eps_qk[:], scale=1.0 / 128.0),
                    reads=["pB", CONST], writes=["rbb"])
                if m3sub < 6:
                    continue
                pp.add("vector", lambda e: e.reciprocal(rkb_sb[:], rbb_sb[:]),
                       reads=["rbb"], writes=["rkb"])
                if m3sub < 7:
                    continue

                def rqtr(e):
                    for t in range(TQ):
                        ins = e.transpose(pC[:, t * 128:(t + 1) * 128],
                                          rkb_sb[:, t * 128:(t + 1) * 128],
                                          identf_sb[:])
                    return ins
                pp.add("tensor", rqtr, reads=["rkb", CONST], writes=["pC"])

                def rqcol(e, hq=hq):
                    for t in range(TQ):
                        ins = e.tensor_scalar_mul(
                            rqc_sb[:, hq * 4 + t:hq * 4 + t + 1],
                            pC[:, t * 128:t * 128 + 1], SCALE / CAP)
                    return ins
                pp.add("vector", rqcol, reads=["pC"], writes=["rqc"])

            if s == 0 and "rqchain" in taps:
                t_s = tapt("sqs", [128, 512], BF16)
                pp.add("gpsimd", lambda e: e.dma_start(out=t_s, in_=sqs_sb[:]),
                       reads=["sqs"], writes=["OUT"], group="g_tap")
                t_r1 = tapt("rbb", [128, 512], F32)
                pp.add("gpsimd", lambda e: e.dma_start(out=t_r1, in_=rbb_sb[:]),
                       reads=["rbb"], writes=["OUT"], group="g_tap")
                t_r2 = tapt("rkb", [128, 512], F32)
                pp.add("gpsimd", lambda e: e.dma_start(out=t_r2, in_=rkb_sb[:]),
                       reads=["rkb"], writes=["OUT"], group="g_tap")
                t_pc = tapt("pCe", [128, 512], F32)
                pp.add("vector", lambda e: e.tensor_copy(rbb_sb[:], pC[:]),
                       reads=["pC"], writes=["rbb"])
                pp.add("gpsimd", lambda e: e.dma_start(out=t_pc, in_=rbb_sb[:]),
                       reads=["rbb"], writes=["OUT"], group="g_tap")
            if s == 0:
                for nm, srcfn, shp, dt_, res in [
                        ("xnT", lambda: xnT_sb[:], [128, DC * 512], BF16, "xnT"),
                        ("knT", lambda: knT_sb[:], [128, KV * 2048], BF16, "knT0"),
                        ("v", lambda: v_sb[:], [128, 16 * 512], BF16, "v0"),
                        ("qT", lambda: qT_sb, [128, 8192], BF16, "qT"),
                        ("rqc", lambda: rqc_sb[:], [128, 64], F32, "rqc")]:
                    if nm in taps:
                        tdst = tapt(nm, shp, dt_)
                        pp.add("gpsimd", (lambda tdst=tdst, srcfn=srcfn:
                                          lambda e: e.dma_start(
                                              out=tdst, in_=srcfn()))(),
                               reads=[res], writes=["OUT"], group="g_tap")

            if maxphase < 4:
                continue
            its = []
            for t in range(TQ):
                tqg = s * TQ + t
                for h in range(HQ):
                    its.append((t, h, tqg))

            def m4_meta(k):
                t, h, tqg = its[k]
                ncols = (tqg + 1) * 128
                return dict(t=t, h=h, g=h // 4, tqg=tqg, ncols=ncols,
                            J=(ncols + 511) // 512, ntk=tqg + 1, par=k % 2,
                            mo=16 * (k % 2))

            def m4_F(k):
                m = m4_meta(k)
                t, h, g, J, ncols, par = (m["t"], m["h"], m["g"], m["J"],
                                          m["ncols"], m["par"])
                TB = TBp[par]
                RTB = f"TB{par}"
                for j in range(J):
                    nj = min(512, ncols - j * 512)
                    PSR = f"p{'AB'[j % 2]}"
                    pp.add("tensor", (lambda h=h, t=t, g=g, j=j, nj=nj:
                                      lambda e: e.matmul(
                        pS[j % 2][:, 0:nj],
                        qT_sb[:, h * 512 + t * 128:h * 512 + (t + 1) * 128],
                        knT_sb[:, g * 2048 + j * 512:g * 2048 + j * 512 + nj],
                        start=True, stop=True))(),
                        reads=["qT"] + knres, writes=[PSR])
                    pp.add("scalar", (lambda h=h, t=t, j=j, nj=nj, TB=TB:
                                      lambda e: e.activation(
                        TB[:, j * 512:j * 512 + nj], pS[j % 2][:, 0:nj],
                        AF.Tanh,
                        scale=rqc_sb[:, h * 4 + t:h * 4 + t + 1]))(),
                        reads=[PSR, "rqc"], writes=[RTB])

            def m4_B1(k):
                m = m4_meta(k)
                ncols, par, mo = m["ncols"], m["par"], m["mo"]
                TB = TBp[par]
                RTB, RMS = f"TB{par}", f"mst{par}"
                pp.add("vector", (lambda ncols=ncols, TB=TB:
                                  lambda e: e.tensor_tensor(
                    TB[:, ncols - 128:ncols], TB[:, ncols - 128:ncols],
                    tri_sb[:], ALU.min))(),
                    reads=[RTB, CONST], writes=[RTB])
                pp.add("vector", (lambda ncols=ncols, TB=TB, mo=mo:
                                  lambda e: e.reduce_max(
                    mst_sb[:, mo + 4:mo + 5], TB[:, 0:ncols], AX.X))(),
                    reads=[RTB], writes=[RMS])
                pp.add("vector", (lambda mo=mo: lambda e: e.tensor_scalar_mul(
                    mst_sb[:, mo + 5:mo + 6], mst_sb[:, mo + 4:mo + 5], -CAP))(),
                    reads=[RMS], writes=[RMS])

            def m4_B2(k):
                m = m4_meta(k)
                ncols, J, par, mo = m["ncols"], m["J"], m["par"], m["mo"]
                TB, PB = TBp[par], PBp[par]
                RTB, RPB = f"TB{par}", f"PB{par}"
                RMS, RMSS = f"mst{par}", f"msts{par}"
                for j in range(J):
                    nj = min(512, ncols - j * 512)
                    pp.add("scalar", (lambda j=j, nj=nj, TB=TB, PB=PB, mo=mo:
                                      lambda e: e.activation(
                        PB[:, j * 512:j * 512 + nj],
                        TB[:, j * 512:j * 512 + nj],
                        AF.Exp, bias=mst_sb[:, mo + 5:mo + 6], scale=CAP,
                        accum_out=mst_sb[:, mo + 8 + j:mo + 9 + j]))(),
                        reads=[RTB, RMS], writes=[RPB, RMSS])

            def m4_B3(k):
                m = m4_meta(k)
                ncols, J, par, mo = m["ncols"], m["J"], m["par"], m["mo"]
                PB = PBp[par]
                RPB = f"PB{par}"
                RMSS, RMSS2 = f"msts{par}", f"msts2{par}"
                if J > 1:
                    pp.add("vector", (lambda J=J, mo=mo: lambda e: e.reduce_sum(
                        mst_sb[:, mo + 12:mo + 13],
                        mst_sb[:, mo + 8:mo + 8 + J], AX.X))(),
                        reads=[RMSS], writes=[RMSS2])
                    scol = 12
                else:
                    scol = 8
                pp.add("vector", (lambda mo=mo, scol=scol: lambda e: e.reciprocal(
                    mst_sb[:, mo + 13:mo + 14],
                    mst_sb[:, mo + scol:mo + scol + 1]))(),
                    reads=[RMSS if J == 1 else RMSS2], writes=[RMSS2])
                pp.add("vector", (lambda ncols=ncols, PB=PB, mo=mo: lambda e:
                                  e.tensor_scalar_mul(
                    PB[:, 0:ncols], PB[:, 0:ncols],
                    mst_sb[:, mo + 13:mo + 14]))(),
                    reads=[RPB, RMSS2], writes=[RPB])

            def m4_B4a(k):
                m = m4_meta(k)
                t, h, g, ntk, par = m["t"], m["h"], m["g"], m["ntk"], m["par"]
                PB, PTB = PBp[par], PTBp[par]
                RPB, RPTB = f"PB{par}", f"PTB{par}"
                for r in range((ntk + 7) // 8):
                    n = min(8, ntk - r * 8)

                    def ptr(e, r=r, n=n, PB=PB):
                        for b in range(n):
                            c = r * 8 + b
                            ins = e.transpose(pT[:, b * 128:(b + 1) * 128],
                                              PB[:, c * 128:(c + 1) * 128],
                                              ident_sb[:])
                        return ins
                    pp.add("tensor", ptr, reads=[RPB, CONST], writes=["pT"])
                    pp.add("vector", (lambda r=r, n=n, PTB=PTB:
                                      lambda e: e.tensor_copy(
                        PTB[:, r * 1024:r * 1024 + n * 128],
                        pT[:, 0:n * 128]))(), reads=["pT"], writes=[RPTB])

                PO = f"pO{h % 2}"

                def pv(e, g=g, ntk=ntk, h=h, PTB=PTB):
                    for c in range(ntk):
                        ins = e.matmul(pO[h % 2],
                                       v_sb[:, c * 512 + g * 128:
                                            c * 512 + (g + 1) * 128],
                                       PTB[:, c * 128:(c + 1) * 128],
                                       start=(c == 0), stop=(c == ntk - 1))
                    return ins
                pp.add("tensor", pv, reads=[RPTB] + vres, writes=[PO])

            def m4_B4b(k):
                m = m4_meta(k)
                t, h = m["t"], m["h"]
                PO = f"pO{h % 2}"
                pp.add("scalar", (lambda h=h, t=t: lambda e: e.copy(
                    OT_sb[:, h * 512 + t * 128:h * 512 + (t + 1) * 128],
                    pO[h % 2]))(), reads=[PO], writes=["OT"])

            N4 = len(its)
            for k in range(N4):
                m4_F(k)
                if k >= 1:
                    m4_B3(k - 1)
                    m4_B4a(k - 1)
                if k >= 2:
                    m4_B4b(k - 2)
                m4_B1(k)
                m4_B2(k)
            m4_B3(N4 - 1)
            m4_B4a(N4 - 1)
            m4_B4b(N4 - 2)
            m4_B4b(N4 - 1)

            if s == 0 and "OT" in taps:
                t6 = tapt("OT", [128, 8192], BF16)
                pp.add("gpsimd", lambda e: e.dma_start(out=t6, in_=OT_sb),
                       reads=["OT"], writes=["OUT"], group="g_tap")

            if maxphase < 5:
                continue
            for ds in range(DS):
                for dqc in range(HQ):
                    if dqc % 4 == 0:
                        wr = next_wst()
                        pp.add("sync", (lambda ds=ds, dqc=dqc, wr=wr:
                                        lambda e: e.dma_start(
                            out=wst_sb[wr][:].rearrange("p (b c) -> p b c", c=512),
                            in_=woT_s[(ds * 16 + dqc) * 128:
                                      (ds * 16 + dqc + 4) * 128, :].rearrange(
                                "(b p) c -> p b c", p=128)))(),
                            reads=["woTs"], writes=[f"wst{wr}"],
                            group=f"g_wst{wr}")
                        cwr = wr

                    def oproj(e, dqc=dqc, wr=cwr):
                        for t in range(TQ):
                            ins = e.matmul(
                                p_out[:, t * 512:(t + 1) * 512],
                                OT_sb[:, dqc * 512 + t * 128:
                                      dqc * 512 + (t + 1) * 128],
                                wst_sb[wr][:, (dqc % 4) * 512:(dqc % 4 + 1) * 512],
                                start=(dqc == 0), stop=(dqc == HQ - 1))
                        return ins
                    pp.add("tensor", oproj, reads=["OT", f"wst{cwr}"],
                           writes=["pO0", "pO1", "pO2", "pO3"])
                for t in range(TQ):
                    pp.add("vector", (lambda ds=ds, t=t: lambda e:
                                      e.scalar_tensor_tensor(
                        xs_sb[:, t * 2048 + ds * 512:t * 2048 + (ds + 1) * 512],
                        p_out[:, t * 512:(t + 1) * 512], 1.0,
                        xs_sb[:, t * 2048 + ds * 512:t * 2048 + (ds + 1) * 512],
                        ALU.mult, ALU.add))(),
                        reads=[f"pO{t}", f"xs{t}"], writes=[f"xs{t}"])

            if s == 0 and "tmp" in taps:
                t7 = tapt("tmp", [128, 8192], F32)
                pp.add("gpsimd", lambda e: e.dma_start(out=t7, in_=xs_sb[:]),
                       reads=["xs0", "xs1", "xs2", "xs3"], writes=["OUT"],
                       group="g_tap")

            if maxphase < 6:
                continue
            for hc in range(GH):
                par = hc % 2
                wr1 = next_wst()
                pp.add("sync", (lambda hc=hc, wr1=wr1: lambda e: e.dma_start(
                    out=wst_sb[wr1][:],
                    in_=wguT_s[hc * 128:(hc + 1) * 128, :]))(),
                    reads=["wguTs"], writes=[f"wst{wr1}"], group=f"g_wst{wr1}")
                wr2 = next_wst()
                pp.add("sync", (lambda hc=hc, wr2=wr2: lambda e: e.dma_start(
                    out=wst_sb[wr2][:],
                    in_=wguT_s[(64 + hc) * 128:(65 + hc) * 128, :]))(),
                    reads=["wguTs"], writes=[f"wst{wr2}"], group=f"g_wst{wr2}")

                def gmm(e, wr1=wr1):
                    for dc in range(DC):
                        ins = e.matmul(pA[:],
                                       wst_sb[wr1][:, dc * 128:(dc + 1) * 128],
                                       xnT_sb[:, dc * 512:(dc + 1) * 512],
                                       start=(dc == 0), stop=(dc == DC - 1))
                    return ins
                pp.add("tensor", gmm, reads=[f"wst{wr1}", "xnT"], writes=["pA"])
                pp.add("scalar", (lambda par=par: lambda e: e.activation(
                    As_sb[0][:], pA[:], AF.Silu))(),
                    reads=["pA"], writes=["As0"])

                def umm(e, wr2=wr2):
                    for dc in range(DC):
                        ins = e.matmul(pB[:],
                                       wst_sb[wr2][:, dc * 128:(dc + 1) * 128],
                                       xnT_sb[:, dc * 512:(dc + 1) * 512],
                                       start=(dc == 0), stop=(dc == DC - 1))
                    return ins
                pp.add("tensor", umm, reads=[f"wst{wr2}", "xnT"], writes=["pB"])
                pp.add("vector", (lambda hc=hc, par=par: lambda e: e.tensor_tensor(
                    big_sb[:, hc * 512:(hc + 1) * 512], As_sb[0][:], pB[:],
                    ALU.mult))(),
                    reads=["As0", "pB"], writes=["H"])
                pp.add("scalar", (lambda hc=hc, par=par: lambda e: e.activation(
                    hsq_sb[0][:], big_sb[:, hc * 512:(hc + 1) * 512],
                    AF.Square))(),
                    reads=["H"], writes=["hsq0"])
                pp.add("tensor", (lambda hc=hc, par=par: lambda e: e.matmul(
                    pC[:], ones_sb[:], hsq_sb[0][:],
                    start=(hc == 0), stop=(hc == GH - 1)))(),
                    reads=["hsq0", CONST], writes=["pC"])
            pp.add("scalar", lambda e: e.activation(
                rbb_sb[:], pC[:], AF.Sqrt, bias=eps_ff[:], scale=1.0 / 8192.0),
                reads=["pC", CONST], writes=["rbb"])
            pp.add("vector", lambda e: e.reciprocal(rkb_sb[:], rbb_sb[:]),
                   reads=["rbb"], writes=["rkb"])

            def invtr(e):
                for t in range(TQ):
                    ins = e.transpose(pA[:, t * 128:(t + 1) * 128],
                                      rkb_sb[:, t * 128:(t + 1) * 128],
                                      identf_sb[:])
                return ins
            pp.add("tensor", invtr, reads=["rkb", CONST], writes=["pA"])

            def invcol(e):
                for t in range(TQ):
                    ins = e.tensor_scalar_mul(invc_sb[:, t:t + 1],
                                              pA[:, t * 128:t * 128 + 1], 1.0)
                return ins
            pp.add("vector", invcol, reads=["pA"], writes=["invc"])

            if s == 0 and "H" in taps:
                t8 = tapt("H", [128, 1024], BF16)
                pp.add("gpsimd", lambda e: e.dma_start(out=t8,
                                                       in_=big_sb[:, 0:1024]),
                       reads=["H"], writes=["OUT"], group="g_tap")
            if s == 0 and "invc" in taps:
                t9 = tapt("invc", [128, 4], F32)
                pp.add("gpsimd", lambda e: e.dma_start(out=t9, in_=invc_sb[:]),
                       reads=["invc"], writes=["OUT"], group="g_tap")

            if maxphase < 7:
                continue
            for ds in range(DS):
                for hc in range(GH):
                    if hc % 4 == 0:
                        wr = next_wst()
                        pp.add("sync", (lambda ds=ds, hc=hc, wr=wr:
                                        lambda e: e.dma_start(
                            out=wst_sb[wr][:].rearrange("p (b c) -> p b c", c=512),
                            in_=wpT_s[(ds * 64 + hc) * 128:
                                      (ds * 64 + hc + 4) * 128, :].rearrange(
                                "(b p) c -> p b c", p=128)))(),
                            reads=["wpTs"], writes=[f"wst{wr}"],
                            group=f"g_wst{wr}")
                        cwr = wr

                    def fproj(e, hc=hc, wr=cwr):
                        for t in range(TQ):
                            ins = e.matmul(
                                p_out[:, t * 512:(t + 1) * 512],
                                big_sb[:, hc * 512 + t * 128:
                                       hc * 512 + (t + 1) * 128],
                                wst_sb[wr][:, (hc % 4) * 512:(hc % 4 + 1) * 512],
                                start=(hc == 0), stop=(hc == GH - 1))
                        return ins
                    pp.add("tensor", fproj, reads=["H", f"wst{cwr}"],
                           writes=["pO0", "pO1", "pO2", "pO3"])
                for t in range(TQ):
                    ob = 0
                    pp.add("vector", (lambda ds=ds, t=t, ob=ob: lambda e:
                                      e.scalar_tensor_tensor(
                        outb_sb[ob][:],
                        p_out[:, t * 512:(t + 1) * 512], invc_sb[:, t:t + 1],
                        xs_sb[:, t * 2048 + ds * 512:t * 2048 + (ds + 1) * 512],
                        ALU.mult, ALU.add))(),
                        reads=[f"pO{t}", "invc", f"xs{t}"], writes=[f"outb{ob}"])
                    pp.add("gpsimd", (lambda s=s, ds=ds, t=t, ob=ob:
                                      lambda e: e.dma_start(
                        out=out_d[s * TS + t * 128:s * TS + (t + 1) * 128,
                                  ds * 512:(ds + 1) * 512],
                        in_=outb_sb[ob][:]))(),
                        reads=[f"outb{ob}"], writes=["OUT"],
                        group=f"g_out{ob}")

        pp.add("vector", lambda e: e.engine_nop(), reads=["OUT"])
        pp.emit_block()

    return nc, dbg


_CACHE = {}


def kernel(L0_attn_x, L0_attn_wq, L0_attn_wk, L0_attn_wv, L0_attn_wo,
           L0_attn_q_gamma, L0_attn_k_gamma, L0_ffn_w_gate_up, L0_ffn_w_proj):
    if "nc" not in _CACHE:
        _CACHE["nc"] = build()[0]
    nc = _CACHE["nc"]
    x = np.asarray(L0_attn_x, dtype=np.float32)
    shared = {
        "wq": np.ascontiguousarray(np.asarray(L0_attn_wq, np.float32)),
        "wk": np.ascontiguousarray(np.asarray(L0_attn_wk, np.float32)),
        "wv": np.ascontiguousarray(np.asarray(L0_attn_wv, np.float32)),
        "wo": np.ascontiguousarray(np.asarray(L0_attn_wo, np.float32)),
        "qg": np.ascontiguousarray(np.asarray(L0_attn_q_gamma, np.float32)),
        "kg": np.ascontiguousarray(np.asarray(L0_attn_k_gamma, np.float32)),
        "wgu": np.ascontiguousarray(np.asarray(L0_ffn_w_gate_up, np.float32)),
        "wp": np.ascontiguousarray(np.asarray(L0_ffn_w_proj, np.float32)),
    }
    in_maps = []
    for b in range(8):
        m = dict(shared)
        m["x"] = np.ascontiguousarray(x[b])
        in_maps.append(m)
    res = run_bass_kernel_spmd(nc, in_maps, core_ids=list(range(8)))
    return np.stack([np.asarray(res.results[b]["out"], np.float32)
                     for b in range(8)], axis=0)


# version 17
# speedup vs baseline: 1.0638x; 1.0105x over previous
import numpy as np
import ml_dtypes
from contextlib import ExitStack

import concourse.bass as bass
import concourse.mybir as mybir
from concourse.bass_utils import run_bass_kernel_spmd

F32 = mybir.dt.float32
BF16 = mybir.dt.bfloat16
AF = mybir.ActivationFunctionType
ALU = mybir.AluOpType
AX = mybir.AxisListType
BF = ml_dtypes.bfloat16

TT = 2048
DD = 2048
DC = 16
HQ = 16
KV = 4
HD = 128
SL = 4
TS = 512
TQ = 4
GH = 64
DS = 4
CAP = 50.0
SCALE = float(HD) ** -0.5
THC = 0.5009765625
EPS_QK = 1e-6
EPS_LN = 1e-5
EPS_FF = float(np.finfo(np.float32).eps)

ENGS = ("sync", "gpsimd", "tensor", "scalar", "vector")


class Pipe:

    COMPUTE = ("tensor", "scalar", "vector")

    def __init__(self, nc, ctx):
        self.nc = nc
        self.ctx = ctx
        self.sems = {}
        self.pos = {}
        self.stream = {e: [] for e in ENGS}
        self.writers = {}
        self.readers = {}

    def _sem(self, name):
        if name not in self.sems:
            self.sems[name] = self.ctx.enter_context(
                self.nc.semaphore("sem_" + name))
            self.pos[name] = 0
        return self.sems[name]

    ALIAS = {"H": ("qT", "TB0", "PB0", "PTB0", "TB1", "PB1", "PTB1", "OT"),
             "qT": ("H",), "TB0": ("H",), "PB0": ("H",), "PTB0": ("H",),
             "TB1": ("H",), "PB1": ("H",), "PTB1": ("H",), "OT": ("H",)}
    PSUMS = ("pA", "pB", "pC", "pT", "pO0", "pO1", "pO2", "pO3",
             "pW0", "pW1")

    def add(self, eng, fn, reads=(), writes=(), group=None):
        is_dma = eng in ("sync", "gpsimd")
        grp = group if group is not None else eng
        assert not (is_dma and grp == eng), \
            f"DMA step on {eng} needs an explicit slot/aggregate group"
        inc = 16 if is_dma else 1
        self._sem(grp)
        waits = {}

        def dep(d, kind):
            for sm, (pe, pv) in d.items():
                if pe == eng:
                    if kind == "raw":
                        if eng == "tensor":
                            continue
                    else:
                        if eng in self.COMPUTE:
                            continue
                waits[sm] = max(waits.get(sm, 0), pv)

        for r in reads:
            for rr in (r,) + self.ALIAS.get(r, ()):
                dep(self.writers.get(rr, {}), "raw")
            if r in self.PSUMS:
                dep(self.readers.get(r, {}), "war")
        for r in writes:
            for rr in (r,) + self.ALIAS.get(r, ()):
                dep(self.readers.get(rr, {}), "war")
                dep(self.writers.get(rr, {}), "waw")

        self.pos[grp] += inc
        after = self.pos[grp]
        for r in reads:
            self.readers.setdefault(r, {})[grp] = (eng, after)
        for r in writes:
            self.writers.setdefault(r, {})[grp] = (eng, after)
        self.stream[eng].append((fn, waits, grp))
        return after

    def drain_dmas(self):
        waits = {sm: self.pos[sm] for sm in self.sems
                 if sm not in ENGS and self.pos[sm] > 0}
        self.pos["vector"] = self.pos.get("vector", 0) + 1
        self._sem("vector")
        self.stream["vector"].append(
            (lambda e: e.engine_nop(), waits, "vector"))

    def emit_block(self):
        nc = self.nc
        sems = self.sems
        streams = {e: list(self.stream[e]) for e in ENGS}
        for e in ENGS:
            self.stream[e] = []
        with nc.Block() as block:
            def mk(name):
                def inner(e):
                    inc = 16 if name in ("sync", "gpsimd") else 1
                    watermark = {}
                    for fn, waits, grp in streams[name]:
                        for sm, pv in sorted(waits.items()):
                            if watermark.get(sm, 0) >= pv:
                                continue
                            e.wait_ge(sems[sm], pv)
                            watermark[sm] = pv
                        fn(e).then_inc(sems[grp], inc)
                return inner
            block.sync(mk("sync"))
            block.gpsimd(mk("gpsimd"))
            block.tensor(mk("tensor"))
            block.scalar(mk("scalar"))
            block.vector(mk("vector"))


def sap(handle, pitch, col0, nblk, stride, width=128):
    return bass.AP(handle, col0, [[pitch, 128], [stride, nblk], [1, width]])


def build(taps=(), nslices=SL, maxphase=7, m3sub=99):
    taps = set(taps)
    nc = bass.Bass("TRN2", target_bir_lowering=False)

    x_d = nc.dram_tensor("x", [TT, DD], F32, kind="ExternalInput").ap()
    wq_d = nc.dram_tensor("wq", [DD, DD], F32, kind="ExternalInput").ap()
    wk_d = nc.dram_tensor("wk", [512, DD], F32, kind="ExternalInput").ap()
    wv_d = nc.dram_tensor("wv", [512, DD], F32, kind="ExternalInput").ap()
    wo_d = nc.dram_tensor("wo", [DD, DD], F32, kind="ExternalInput").ap()
    qg_d = nc.dram_tensor("qg", [HD], F32, kind="ExternalInput").ap()
    kg_d = nc.dram_tensor("kg", [HD], F32, kind="ExternalInput").ap()
    wgu_d = nc.dram_tensor("wgu", [16384, DD], F32, kind="ExternalInput").ap()
    wp_d = nc.dram_tensor("wp", [DD, 8192], F32, kind="ExternalInput").ap()
    out_d = nc.dram_tensor("out", [TT, DD], F32, kind="ExternalOutput").ap()

    wqT_s = nc.dram_tensor("wqT_s", [16 * 128, 2048], BF16, kind="Internal").ap()
    wguT_s = nc.dram_tensor("wguT_s", [128 * 128, 2048], BF16, kind="Internal").ap()
    woT_s = nc.dram_tensor("woT_s", [4 * 16 * 128, 512], BF16, kind="Internal").ap()
    wpT_s = nc.dram_tensor("wpT_s", [4 * 64 * 128, 512], BF16, kind="Internal").ap()

    ident_np = np.eye(128, dtype=BF)
    identf_np = np.eye(128, dtype=np.float32)
    ones_np = np.ones((128, 128), dtype=BF)
    tri_np = np.where(np.arange(128)[None, :] <= np.arange(128)[:, None],
                      np.float32(9.0), np.float32(-1e5)).astype(np.float32)
    ident_d = nc.inline_tensor(ident_np, name="identc").ap()
    identf_d = nc.inline_tensor(identf_np, name="identfc").ap()
    ones_d = nc.inline_tensor(ones_np, name="onesc").ap()
    tri_d = nc.inline_tensor(tri_np, name="tric").ap()

    dbg = {}

    def tapt(name, shape, dt):
        ap = nc.dram_tensor("tap_" + name, list(shape), dt,
                            kind="ExternalOutput").ap()
        dbg[name] = ap
        return ap

    with ExitStack() as ctx:
        pp = Pipe(nc, ctx)

        sb = lambda n_, s_, d_: ctx.enter_context(nc.sbuf_tensor(n_, s_, d_))
        ident_sb = sb("ident_sb", [128, 128], BF16)
        identf_sb = sb("identf_sb", [128, 128], F32)
        ones_sb = sb("ones_sb", [128, 128], BF16)
        tri_sb = sb("tri_sb", [128, 128], F32)
        gq_sb = sb("gq_sb", [128, 1], F32)
        gk_sb = sb("gk_sb", [128, 1], F32)
        eps_ln = sb("eps_ln", [128, 1], F32)
        eps_qk = sb("eps_qk", [128, 1], F32)
        eps_ff = sb("eps_ff", [128, 1], F32)
        wkT_sb = sb("wkT_sb", [128, 8192], BF16)
        wvT_sb = sb("wvT_sb", [128, 8192], BF16)

        CONST = "consts"
        pp.add("sync", lambda e: e.dma_start(out=ident_sb[:], in_=ident_d),
               writes=[CONST], group="g_const")
        pp.add("sync", lambda e: e.dma_start(out=identf_sb[:], in_=identf_d),
               writes=[CONST], group="g_const")
        pp.add("sync", lambda e: e.dma_start(out=ones_sb[:], in_=ones_d),
               writes=[CONST], group="g_const")
        pp.add("sync", lambda e: e.dma_start(out=tri_sb[:], in_=tri_d),
               writes=[CONST], group="g_const")
        pp.add("sync", lambda e: e.dma_start(
            out=gq_sb[:], in_=qg_d.rearrange("(p o) -> p o", o=1)), writes=[CONST], group="g_const")
        pp.add("sync", lambda e: e.dma_start(
            out=gk_sb[:], in_=kg_d.rearrange("(p o) -> p o", o=1)), writes=[CONST], group="g_const")
        pp.add("vector", lambda e: e.memset(eps_ln[:], EPS_LN), writes=[CONST])
        pp.add("vector", lambda e: e.memset(eps_qk[:], EPS_QK), writes=[CONST])
        pp.add("vector", lambda e: e.memset(eps_ff[:], EPS_FF), writes=[CONST])

        with ExitStack() as wctx:
            wsb = lambda n_, s_, d_: wctx.enter_context(nc.sbuf_tensor(n_, s_, d_))
            NWS = 3
            wnat = [wsb(f"wnat{i}", [128, 2048], BF16) for i in range(NWS)]
            what = [wsb(f"what{i}", [128, 2048], BF16) for i in range(NWS)]
            u_t = wsb("u_t", [128, 2048], BF16)
            sg_t = [wsb(f"sg_t{i}", [128, 2048], BF16) for i in range(NWS)]
            aw_t = [wsb(f"aw_t{i}", [128, 2048], BF16) for i in range(NWS)]
            gsum = wsb("gsum", [128, 128], F32)
            sc_bf = wsb("sc_bf", [128, 128], BF16)
            th_f = wsb("th_f", [128, 128], F32)
            pstage = [wsb(f"pstage{i}", [128, 2048], BF16) for i in range(NWS)]
            dstage = wsb("dstage", [128, 16 * 512], BF16)
            pW = [wctx.enter_context(nc.psum_tensor(f"pW{i}", [128, 1024], BF16))
                  for i in range(2)]
            rctr = [0]

            def quant_a(wcols, sl):
                WN = f"wnat{sl}"
                pp.add("scalar", (lambda sl=sl, wcols=wcols: lambda e: e.activation(
                    aw_t[sl][:, 0:wcols], wnat[sl][:, 0:wcols], AF.Abs))(),
                    reads=[WN], writes=[f"awt{sl}"])
                pp.add("scalar", (lambda sl=sl, wcols=wcols: lambda e: e.activation(
                    sg_t[sl][:, 0:wcols], wnat[sl][:, 0:wcols], AF.Sign))(),
                    reads=[WN], writes=[f"sgt{sl}"])

            def quant_steps(wcols, sl):
                G = wcols // 64
                WN = f"wnat{sl}"
                WH = f"what{sl}"
                w3 = wnat[sl][:, 0:wcols].rearrange("p (g k) -> p g k", k=64)
                thb = th_f[:, 0:G].unsqueeze(2).broadcast_to([128, G, 64])
                scb = sc_bf[:, 0:G].unsqueeze(2).broadcast_to([128, G, 64])
                u3 = u_t[:, 0:wcols].rearrange("p (g k) -> p g k", k=64)
                s3 = sg_t[sl][:, 0:wcols].rearrange("p (g k) -> p g k", k=64)
                a3 = aw_t[sl][:, 0:wcols].rearrange("p (g k) -> p g k", k=64)
                h3 = what[sl][:, 0:wcols].rearrange("p (g k) -> p g k", k=64)
                pp.add("vector", lambda e: e.tensor_reduce(
                    gsum[:, 0:G], w3, AX.X, ALU.add, apply_absolute_value=True),
                    reads=[WN], writes=["gsum"])
                pp.add("vector", lambda e: e.tensor_scalar(
                    sc_bf[:, 0:G], gsum[:, 0:G], 1.0 / 64.0, 1e-8,
                    ALU.mult, ALU.max),
                    reads=["gsum"], writes=["scbf"])
                pp.add("vector", lambda e: e.tensor_scalar_mul(
                    th_f[:, 0:G], sc_bf[:, 0:G], THC),
                    reads=["scbf"], writes=["thf"])
                pp.add("vector", (lambda sl=sl: lambda e: e.tensor_tensor(
                    u3, a3, thb, ALU.is_gt))(),
                    reads=[f"awt{sl}", "thf"], writes=["ut"])
                pp.add("vector", (lambda sl=sl: lambda e: e.tensor_tensor(
                    u3, u3, s3, ALU.mult))(),
                    reads=[f"ut", f"sgt{sl}"], writes=["ut"])
                pp.add("vector", lambda e: e.tensor_tensor(h3, u3, scb, ALU.mult),
                       reads=["ut", "scbf"], writes=[WH])

            def wclass(src, ntiles, wcols, quant, mode, dst=None, dstres=None,
                       dst_sb=None, dst_pitch=None, colmap=None, sbres=None,
                       srcfn=None, wrowfn=None):
                nb = wcols // 128

                def rest(i, sl):
                    WN = f"wnat{sl}"
                    WH = f"what{sl}"
                    PS = f"pstage{sl}"
                    if quant:
                        quant_steps(wcols, sl)
                    srcT = what[sl] if quant else wnat[sl]
                    SR = WH if quant else WN
                    for r in range((nb + 7) // 8):
                        n = min(8, nb - r * 8)

                        pwp = rctr[0] % 2
                        rctr[0] += 1

                        def trn(e, r=r, n=n, srcT=srcT, pwp=pwp):
                            for b in range(n):
                                blk = r * 8 + b
                                ins = e.transpose(
                                    pW[pwp][:, b * 128:(b + 1) * 128],
                                    srcT[:, blk * 128:(blk + 1) * 128],
                                    ident_sb[:])
                            return ins
                        pp.add("tensor", trn, reads=[SR, CONST],
                               writes=[f"pW{pwp}"])

                        if mode == "panel":
                            dfn = (lambda r=r, n=n, sl=sl: lambda: (
                                pstage[sl][:, r * 1024:r * 1024 + n * 128]))()
                            wres = PS
                        elif mode == "sbuf":
                            dfn = (lambda i=i, r=r, n=n: lambda: sap(
                                dst_sb, dst_pitch, colmap(i, r), n,
                                colmap("stride", 0)))()
                            wres = sbres
                        else:
                            dfn = (lambda i=i, r=r, n=n: lambda: sap(
                                dstage, 16 * 512, (r * 8) * 512 + (i % 4) * 128,
                                n, 512))()
                            wres = f"dstage{r}"
                        pp.add("scalar", (lambda dfn=dfn, n=n, pwp=pwp:
                                          lambda e: e.copy(
                            dfn(), pW[pwp][:, 0:n * 128]))(),
                            reads=[f"pW{pwp}"], writes=[wres])
                        if mode == "dsgroup" and i % 4 == 3:
                            for hb in range(r * 8, r * 8 + n):
                                row = (wrowfn(i, hb) if wrowfn
                                       else (i // 4) * nb + hb)
                                pp.add("sync", (lambda row=row, hb=hb:
                                                lambda e: e.dma_start(
                                    out=dst[row * 128:(row + 1) * 128, :],
                                    in_=dstage[:, hb * 512:(hb + 1) * 512]))(),
                                    reads=[f"dstage{r}"], writes=[dstres],
                                    group="g_" + dstres)
                    if mode == "panel":
                        pp.add("sync", (lambda i=i, sl=sl, wcols=wcols:
                                        lambda e: e.dma_start(
                            out=dst[i * 128:(i + 1) * 128, :],
                            in_=pstage[sl][:, 0:wcols]))(),
                            reads=[PS], writes=[dstres], group=f"g_ws{sl}")

                look = 2 if quant else 1
                pending = []
                for i in list(range(ntiles)) + [None] * look:
                    if i is not None:
                        sl = i % NWS
                        WN = f"wnat{sl}"
                        if srcfn is None:
                            sfn = (lambda i=i: lambda:
                                   src[i * 128:(i + 1) * 128, :])()
                        else:
                            sfn = (lambda i=i: lambda: srcfn(i))()
                        pp.add("gpsimd", (lambda sfn=sfn, sl=sl, wcols=wcols:
                                          lambda e: e.dma_start(
                            out=wnat[sl][:, 0:wcols], in_=sfn()))(), writes=[WN],
                            group=f"g_{WN}")
                        if quant:
                            quant_a(wcols, sl)
                    if len(pending) >= look or (i is None and pending):
                        pending.pop(0)()
                    if i is not None:
                        pending.append((lambda i=i, sl=sl:
                                        lambda: rest(i, sl))())

            def kmap(i, r=None):
                if i == "stride":
                    return 128
                return i * 2048 + (r * 8) * 128

            def vmap(i, r=None):
                if i == "stride":
                    return 512
                return (r * 8) * 512 + i * 128

            wclass(wk_d, 4, 2048, False, "sbuf", dst_sb=wkT_sb, dst_pitch=8192,
                   colmap=kmap, sbres="wkTsb")
            wclass(wv_d, 4, 2048, False, "sbuf", dst_sb=wvT_sb, dst_pitch=8192,
                   colmap=vmap, sbres="wvTsb")
            wclass(wq_d, 16, 2048, False, "panel", dst=wqT_s, dstres="wqTs")
            wclass(wgu_d, 128, 2048, True, "panel", dst=wguT_s, dstres="wguTs")
            wclass(wo_d, 16, 2048, False, "dsgroup", dst=woT_s, dstres="woTs")
            def wp_src(i):
                g, cc, ii = i // 16, (i // 4) % 4, i % 4
                row = (g * 4 + ii) * 128
                return wp_d[row:row + 128, cc * 2048:(cc + 1) * 2048]

            def wp_row(i, hb):
                g, cc = i // 16, (i // 4) % 4
                return g * 64 + cc * 16 + hb
            wclass(None, 64, 2048, True, "dsgroup", dst=wpT_s, dstres="wpTs",
                   srcfn=wp_src, wrowfn=wp_row)

            for nm, src_, shp, res in [
                    ("wguT0", lambda: wguT_s[0:128, :], [128, 2048], "wguTs"),
                    ("wkT", lambda: wkT_sb[:], [128, 8192], "wkTsb"),
                    ("wvT", lambda: wvT_sb[:], [128, 8192], "wvTsb"),
                    ("wpT0", lambda: wpT_s[0:128, :], [128, 512], "wpTs"),
                    ("woT0", lambda: woT_s[0:128, :], [128, 512], "woTs"),
                    ("wqT0", lambda: wqT_s[0:128, :], [128, 2048], "wqTs")]:
                if nm in taps:
                    tdst = tapt(nm, shp, BF16)
                    pp.add("gpsimd", (lambda tdst=tdst, src_=src_:
                                      lambda e: e.dma_start(
                                          out=tdst, in_=src_()))(),
                           reads=[res], writes=["OUT"], group="g_tap")

            pp.drain_dmas()
            pp.emit_block()

        xs_sb = sb("xs_sb", [128, 4 * 2048], F32)
        xnT_sb = sb("xnT_sb", [128, DC * 512], BF16)
        knT_sb = sb("knT_sb", [128, KV * 2048], BF16)
        v_sb = sb("v_sb", [128, 16 * 512], BF16)
        big_sb = sb("big_sb", [128, 32768], BF16)
        sqs_sb = sb("sqs_sb", [128, 512], BF16)
        rbb_sb = sb("rbb_sb", [128, 512], F32)
        rkb_sb = sb("rkb_sb", [128, 512], F32)
        rqc_sb = sb("rqc_sb", [128, 64], F32)
        mst_sb = sb("mst_sb", [128, 32], F32)
        stat_sb = sb("stat_sb", [128, 8], F32)
        invc_sb = sb("invc_sb", [128, 4], F32)
        As_sb = [sb("As0", [128, 512], F32)]
        hsq_sb = [sb("hsq0", [128, 512], BF16)]
        outb_sb = [sb("outb0", [128, 512], F32)]
        NWST = 4
        wst_sb = [sb(f"wst{i}", [128, 2048], BF16) for i in range(NWST)]
        wring = [0]

        def next_wst():
            r_ = wring[0] % NWST
            wring[0] += 1
            return r_

        qT_sb = big_sb[:, 0:8192]
        OT_sb = big_sb[:, 8192:16384]
        TBp = [big_sb[:, 16384:20480].bitcast(F32),
               big_sb[:, 24576:28672].bitcast(F32)]
        PBp = [big_sb[:, 20480:22528], big_sb[:, 28672:30720]]
        PTBp = [big_sb[:, 22528:24576], big_sb[:, 30720:32768]]

        p_out = ctx.enter_context(nc.psum_tensor("p_out", [128, 2048], F32))
        pA = ctx.enter_context(nc.psum_tensor("pA", [128, 512], F32))
        pB = ctx.enter_context(nc.psum_tensor("pB", [128, 512], F32))
        pC = ctx.enter_context(nc.psum_tensor("pC", [128, 512], F32))
        pT = ctx.enter_context(nc.psum_tensor("pT", [128, 1024], BF16))
        pS = [pA, pB]
        pO = [p_out[:, 0:128], p_out[:, 512:640]]

        for s in range(nslices):
            tok0 = s * TS
            knres = [f"knT{ss}" for ss in range(s + 1)]
            vres = [f"v{ss}" for ss in range(s + 1)]

            for t in range(TQ):
                XS = f"xs{t}"
                xsl = xs_sb[:, t * 2048:(t + 1) * 2048]
                pp.add("gpsimd", (lambda t=t, tok0=tok0: lambda e: e.dma_start(
                    out=xs_sb[:, t * 2048:(t + 1) * 2048],
                    in_=x_d[tok0 + t * 128:tok0 + (t + 1) * 128, :]))(),
                    writes=[XS], group=f"g_xs{t}")
                par = t % 2
                pp.add("scalar", (lambda xsl=xsl, par=par: lambda e: e.activation(
                    TBp[par][:, :], xsl, AF.Square, accum_out=stat_sb[:, 1:2]))(),
                    reads=[XS], writes=[f"TB{par}", "stat"])
                pp.add("vector", (lambda xsl=xsl: lambda e: e.reduce_sum(
                    stat_sb[:, 0:1], xsl, AX.X))(), reads=[XS], writes=["stat"])

                def stats(e):
                    e.tensor_scalar_mul(stat_sb[:, 2:3], stat_sb[:, 0:1],
                                        1.0 / 2048.0)
                    return e.tensor_scalar_mul(stat_sb[:, 5:6], stat_sb[:, 1:2],
                                               1.0 / 2048.0)
                pp.add("vector", stats, reads=["stat"], writes=["stat"])
                pp.add("vector", lambda e: e.tensor_tensor(
                    stat_sb[:, 6:7], stat_sb[:, 2:3], stat_sb[:, 2:3], ALU.mult),
                    reads=["stat"], writes=["stat"])
                pp.add("vector", lambda e: e.tensor_tensor(
                    stat_sb[:, 3:4], stat_sb[:, 5:6], stat_sb[:, 6:7],
                    ALU.subtract), reads=["stat"], writes=["stat"])
                pp.add("scalar", lambda e: e.activation(
                    stat_sb[:, 4:5], stat_sb[:, 3:4], AF.Sqrt, bias=eps_ln[:]),
                    reads=["stat", CONST], writes=["stat"])
                pp.add("vector", lambda e: e.reciprocal(stat_sb[:, 7:8],
                                                        stat_sb[:, 4:5]),
                       reads=["stat"], writes=["stat"])
                pp.add("vector", (lambda xsl=xsl, par=par: lambda e: e.tensor_scalar(
                    TBp[par].bitcast(BF16)[:, 0:2048], xsl, stat_sb[:, 2:3],
                    stat_sb[:, 7:8], ALU.subtract, ALU.mult))(),
                    reads=[XS, "stat"], writes=[f"TB{par}"])
                for r in range(2):
                    def trn(e, r=r, par=par):
                        xnb = TBp[par].bitcast(BF16)
                        for b in range(8):
                            ins = e.transpose(pT[:, b * 128:(b + 1) * 128],
                                              xnb[:, (r * 8 + b) * 128:
                                                  (r * 8 + b + 1) * 128],
                                              ident_sb[:])
                        return ins
                    pp.add("tensor", trn, reads=[f"TB{par}", CONST], writes=["pT"])
                    pp.add("scalar", (lambda t=t, r=r: lambda e: e.copy(
                        sap(xnT_sb, DC * 512, (r * 8) * 512 + t * 128, 8, 512),
                        pT[:, 0:1024]))(), reads=["pT"], writes=["xnT"])

            if maxphase < 2:
                continue
            for kvc in range(KV):
                def kproj(e, kvc=kvc):
                    for dc in range(DC):
                        ins = e.matmul(pA[:],
                                       wkT_sb[:, kvc * 2048 + dc * 128:
                                              kvc * 2048 + (dc + 1) * 128],
                                       xnT_sb[:, dc * 512:(dc + 1) * 512],
                                       start=(dc == 0), stop=(dc == DC - 1))
                    return ins
                pp.add("tensor", kproj, reads=["wkTsb", "xnT"], writes=["pA"])
                pp.add("scalar", lambda e: e.activation(sqs_sb[:], pA[:],
                                                        AF.Square),
                       reads=["pA"], writes=["sqs"])
                pp.add("tensor", lambda e: e.matmul(pB[:], ones_sb[:], sqs_sb[:],
                                                    start=True, stop=True),
                       reads=["sqs", CONST], writes=["pB"])
                pp.add("scalar", lambda e: e.activation(
                    rbb_sb[:], pB[:], AF.Sqrt, bias=eps_qk[:], scale=1.0 / 128.0),
                    reads=["pB", CONST], writes=["rbb"])
                pp.add("vector", lambda e: e.reciprocal(rkb_sb[:], rbb_sb[:]),
                       reads=["rbb"], writes=["rkb"])
                pp.add("vector", (lambda kvc=kvc, tok0=tok0: lambda e:
                                  e.scalar_tensor_tensor(
                    knT_sb[:, kvc * 2048 + tok0:kvc * 2048 + tok0 + TS],
                    pA[:], gk_sb[:], rkb_sb[:], ALU.mult, ALU.mult))(),
                    reads=["pA", "rkb", CONST], writes=[f"knT{s}"])
            for t in range(TQ):
                def vproj(e, t=t):
                    for dc in range(DC):
                        ins = e.matmul(pC[:],
                                       xnT_sb[:, dc * 512 + t * 128:
                                              dc * 512 + (t + 1) * 128],
                                       wvT_sb[:, dc * 512:(dc + 1) * 512],
                                       start=(dc == 0), stop=(dc == DC - 1))
                    return ins
                pp.add("tensor", vproj, reads=["xnT", "wvTsb"], writes=["pC"])
                pp.add("vector", (lambda s=s, t=t: lambda e: e.tensor_copy(
                    v_sb[:, (s * TQ + t) * 512:(s * TQ + t + 1) * 512], pC[:]))(),
                    reads=["pC"], writes=[f"v{s}"])

            if maxphase < 3:
                continue
            for hq in range(HQ):
                wr = next_wst()
                WR = f"wst{wr}"
                pp.add("sync", (lambda hq=hq, wr=wr: lambda e: e.dma_start(
                    out=wst_sb[wr][:], in_=wqT_s[hq * 128:(hq + 1) * 128, :]))(),
                    reads=["wqTs"], writes=[WR], group=f"g_wst{wr}")

                if m3sub < 1:
                    continue
                def qproj(e, wr=wr):
                    for dc in range(DC):
                        ins = e.matmul(pA[:],
                                       wst_sb[wr][:, dc * 128:(dc + 1) * 128],
                                       xnT_sb[:, dc * 512:(dc + 1) * 512],
                                       start=(dc == 0), stop=(dc == DC - 1))
                    return ins
                pp.add("tensor", qproj, reads=[WR, "xnT"], writes=["pA"])
                if m3sub < 2:
                    continue
                pp.add("vector", (lambda hq=hq: lambda e: e.tensor_scalar_mul(
                    qT_sb[:, hq * 512:(hq + 1) * 512], pA[:], gq_sb[:]))(),
                    reads=["pA", CONST], writes=["qT"])
                if m3sub < 3:
                    continue
                pp.add("scalar", lambda e: e.activation(sqs_sb[:], pA[:],
                                                        AF.Square),
                       reads=["pA"], writes=["sqs"])
                if m3sub < 4:
                    continue
                pp.add("tensor", lambda e: e.matmul(pB[:], ones_sb[:], sqs_sb[:],
                                                    start=True, stop=True),
                       reads=["sqs", CONST], writes=["pB"])
                if m3sub < 5:
                    continue
                pp.add("scalar", lambda e: e.activation(
                    rbb_sb[:], pB[:], AF.Sqrt, bias=eps_qk[:], scale=1.0 / 128.0),
                    reads=["pB", CONST], writes=["rbb"])
                if m3sub < 6:
                    continue
                pp.add("vector", lambda e: e.reciprocal(rkb_sb[:], rbb_sb[:]),
                       reads=["rbb"], writes=["rkb"])
                if m3sub < 7:
                    continue

                def rqtr(e):
                    for t in range(TQ):
                        ins = e.transpose(pC[:, t * 128:(t + 1) * 128],
                                          rkb_sb[:, t * 128:(t + 1) * 128],
                                          identf_sb[:])
                    return ins
                pp.add("tensor", rqtr, reads=["rkb", CONST], writes=["pC"])

                def rqcol(e, hq=hq):
                    for t in range(TQ):
                        ins = e.tensor_scalar_mul(
                            rqc_sb[:, hq * 4 + t:hq * 4 + t + 1],
                            pC[:, t * 128:t * 128 + 1], SCALE / CAP)
                    return ins
                pp.add("vector", rqcol, reads=["pC"], writes=["rqc"])

            if s == 0 and "rqchain" in taps:
                t_s = tapt("sqs", [128, 512], BF16)
                pp.add("gpsimd", lambda e: e.dma_start(out=t_s, in_=sqs_sb[:]),
                       reads=["sqs"], writes=["OUT"], group="g_tap")
                t_r1 = tapt("rbb", [128, 512], F32)
                pp.add("gpsimd", lambda e: e.dma_start(out=t_r1, in_=rbb_sb[:]),
                       reads=["rbb"], writes=["OUT"], group="g_tap")
                t_r2 = tapt("rkb", [128, 512], F32)
                pp.add("gpsimd", lambda e: e.dma_start(out=t_r2, in_=rkb_sb[:]),
                       reads=["rkb"], writes=["OUT"], group="g_tap")
                t_pc = tapt("pCe", [128, 512], F32)
                pp.add("vector", lambda e: e.tensor_copy(rbb_sb[:], pC[:]),
                       reads=["pC"], writes=["rbb"])
                pp.add("gpsimd", lambda e: e.dma_start(out=t_pc, in_=rbb_sb[:]),
                       reads=["rbb"], writes=["OUT"], group="g_tap")
            if s == 0:
                for nm, srcfn, shp, dt_, res in [
                        ("xnT", lambda: xnT_sb[:], [128, DC * 512], BF16, "xnT"),
                        ("knT", lambda: knT_sb[:], [128, KV * 2048], BF16, "knT0"),
                        ("v", lambda: v_sb[:], [128, 16 * 512], BF16, "v0"),
                        ("qT", lambda: qT_sb, [128, 8192], BF16, "qT"),
                        ("rqc", lambda: rqc_sb[:], [128, 64], F32, "rqc")]:
                    if nm in taps:
                        tdst = tapt(nm, shp, dt_)
                        pp.add("gpsimd", (lambda tdst=tdst, srcfn=srcfn:
                                          lambda e: e.dma_start(
                                              out=tdst, in_=srcfn()))(),
                               reads=[res], writes=["OUT"], group="g_tap")

            if maxphase < 4:
                continue
            its = []
            for t in range(TQ):
                tqg = s * TQ + t
                for h in range(HQ):
                    its.append((t, h, tqg))

            def m4_meta(k):
                t, h, tqg = its[k]
                ncols = (tqg + 1) * 128
                return dict(t=t, h=h, g=h // 4, tqg=tqg, ncols=ncols,
                            J=(ncols + 511) // 512, ntk=tqg + 1, par=k % 2,
                            mo=16 * (k % 2))

            def m4_F(k):
                m = m4_meta(k)
                t, h, g, J, ncols, par = (m["t"], m["h"], m["g"], m["J"],
                                          m["ncols"], m["par"])
                TB = TBp[par]
                RTB = f"TB{par}"
                for j in range(J):
                    nj = min(512, ncols - j * 512)
                    PSR = f"p{'AB'[j % 2]}"
                    pp.add("tensor", (lambda h=h, t=t, g=g, j=j, nj=nj:
                                      lambda e: e.matmul(
                        pS[j % 2][:, 0:nj],
                        qT_sb[:, h * 512 + t * 128:h * 512 + (t + 1) * 128],
                        knT_sb[:, g * 2048 + j * 512:g * 2048 + j * 512 + nj],
                        start=True, stop=True))(),
                        reads=["qT"] + knres, writes=[PSR])
                    pp.add("scalar", (lambda h=h, t=t, j=j, nj=nj, TB=TB:
                                      lambda e: e.activation(
                        TB[:, j * 512:j * 512 + nj], pS[j % 2][:, 0:nj],
                        AF.Tanh,
                        scale=rqc_sb[:, h * 4 + t:h * 4 + t + 1]))(),
                        reads=[PSR, "rqc"], writes=[RTB])

            def m4_B1(k):
                m = m4_meta(k)
                ncols, par, mo = m["ncols"], m["par"], m["mo"]
                TB = TBp[par]
                RTB, RMS = f"TB{par}", f"mst{par}"
                pp.add("vector", (lambda ncols=ncols, TB=TB:
                                  lambda e: e.tensor_tensor(
                    TB[:, ncols - 128:ncols], TB[:, ncols - 128:ncols],
                    tri_sb[:], ALU.min))(),
                    reads=[RTB, CONST], writes=[RTB])
                pp.add("vector", (lambda ncols=ncols, TB=TB, mo=mo:
                                  lambda e: e.reduce_max(
                    mst_sb[:, mo + 4:mo + 5], TB[:, 0:ncols], AX.X))(),
                    reads=[RTB], writes=[RMS])
                pp.add("vector", (lambda mo=mo: lambda e: e.tensor_scalar_mul(
                    mst_sb[:, mo + 5:mo + 6], mst_sb[:, mo + 4:mo + 5], -CAP))(),
                    reads=[RMS], writes=[RMS])

            def m4_B2(k):
                m = m4_meta(k)
                ncols, J, par, mo = m["ncols"], m["J"], m["par"], m["mo"]
                TB, PB = TBp[par], PBp[par]
                RTB, RPB = f"TB{par}", f"PB{par}"
                RMS, RMSS = f"mst{par}", f"msts{par}"
                for j in range(J):
                    nj = min(512, ncols - j * 512)
                    pp.add("scalar", (lambda j=j, nj=nj, TB=TB, PB=PB, mo=mo:
                                      lambda e: e.activation(
                        PB[:, j * 512:j * 512 + nj],
                        TB[:, j * 512:j * 512 + nj],
                        AF.Exp, bias=mst_sb[:, mo + 5:mo + 6], scale=CAP,
                        accum_out=mst_sb[:, mo + 8 + j:mo + 9 + j]))(),
                        reads=[RTB, RMS], writes=[RPB, RMSS])

            def m4_B3(k):
                m = m4_meta(k)
                ncols, J, par, mo = m["ncols"], m["J"], m["par"], m["mo"]
                PB = PBp[par]
                RPB = f"PB{par}"
                RMSS, RMSS2 = f"msts{par}", f"msts2{par}"
                if J > 1:
                    pp.add("vector", (lambda J=J, mo=mo: lambda e: e.reduce_sum(
                        mst_sb[:, mo + 12:mo + 13],
                        mst_sb[:, mo + 8:mo + 8 + J], AX.X))(),
                        reads=[RMSS], writes=[RMSS2])
                    scol = 12
                else:
                    scol = 8
                pp.add("vector", (lambda mo=mo, scol=scol: lambda e: e.reciprocal(
                    mst_sb[:, mo + 13:mo + 14],
                    mst_sb[:, mo + scol:mo + scol + 1]))(),
                    reads=[RMSS if J == 1 else RMSS2], writes=[RMSS2])
                pp.add("vector", (lambda ncols=ncols, PB=PB, mo=mo: lambda e:
                                  e.tensor_scalar_mul(
                    PB[:, 0:ncols], PB[:, 0:ncols],
                    mst_sb[:, mo + 13:mo + 14]))(),
                    reads=[RPB, RMSS2], writes=[RPB])

            def m4_B4a(k):
                m = m4_meta(k)
                t, h, g, ntk, par = m["t"], m["h"], m["g"], m["ntk"], m["par"]
                PB, PTB = PBp[par], PTBp[par]
                RPB, RPTB = f"PB{par}", f"PTB{par}"
                for r in range((ntk + 7) // 8):
                    n = min(8, ntk - r * 8)

                    def ptr(e, r=r, n=n, PB=PB):
                        for b in range(n):
                            c = r * 8 + b
                            ins = e.transpose(pT[:, b * 128:(b + 1) * 128],
                                              PB[:, c * 128:(c + 1) * 128],
                                              ident_sb[:])
                        return ins
                    pp.add("tensor", ptr, reads=[RPB, CONST], writes=["pT"])
                    pp.add("vector", (lambda r=r, n=n, PTB=PTB:
                                      lambda e: e.tensor_copy(
                        PTB[:, r * 1024:r * 1024 + n * 128],
                        pT[:, 0:n * 128]))(), reads=["pT"], writes=[RPTB])

                PO = f"pO{h % 2}"

                def pv(e, g=g, ntk=ntk, h=h, PTB=PTB):
                    for c in range(ntk):
                        ins = e.matmul(pO[h % 2],
                                       v_sb[:, c * 512 + g * 128:
                                            c * 512 + (g + 1) * 128],
                                       PTB[:, c * 128:(c + 1) * 128],
                                       start=(c == 0), stop=(c == ntk - 1))
                    return ins
                pp.add("tensor", pv, reads=[RPTB] + vres, writes=[PO])

            def m4_B4b(k):
                m = m4_meta(k)
                t, h = m["t"], m["h"]
                PO = f"pO{h % 2}"
                pp.add("scalar", (lambda h=h, t=t: lambda e: e.copy(
                    OT_sb[:, h * 512 + t * 128:h * 512 + (t + 1) * 128],
                    pO[h % 2]))(), reads=[PO], writes=["OT"])

            N4 = len(its)
            for k in range(N4):
                m4_F(k)
                if k >= 1:
                    m4_B3(k - 1)
                    m4_B4a(k - 1)
                if k >= 2:
                    m4_B4b(k - 2)
                m4_B1(k)
                m4_B2(k)
            m4_B3(N4 - 1)
            m4_B4a(N4 - 1)
            m4_B4b(N4 - 2)
            m4_B4b(N4 - 1)

            if s == 0 and "OT" in taps:
                t6 = tapt("OT", [128, 8192], BF16)
                pp.add("gpsimd", lambda e: e.dma_start(out=t6, in_=OT_sb),
                       reads=["OT"], writes=["OUT"], group="g_tap")

            if maxphase < 5:
                continue
            for ds in range(DS):
                for dqc in range(HQ):
                    if dqc % 4 == 0:
                        wr = next_wst()
                        pp.add("sync", (lambda ds=ds, dqc=dqc, wr=wr:
                                        lambda e: e.dma_start(
                            out=wst_sb[wr][:].rearrange("p (b c) -> p b c", c=512),
                            in_=woT_s[(ds * 16 + dqc) * 128:
                                      (ds * 16 + dqc + 4) * 128, :].rearrange(
                                "(b p) c -> p b c", p=128)))(),
                            reads=["woTs"], writes=[f"wst{wr}"],
                            group=f"g_wst{wr}")
                        cwr = wr

                    def oproj(e, dqc=dqc, wr=cwr):
                        for t in range(TQ):
                            ins = e.matmul(
                                p_out[:, t * 512:(t + 1) * 512],
                                OT_sb[:, dqc * 512 + t * 128:
                                      dqc * 512 + (t + 1) * 128],
                                wst_sb[wr][:, (dqc % 4) * 512:(dqc % 4 + 1) * 512],
                                start=(dqc == 0), stop=(dqc == HQ - 1))
                        return ins
                    pp.add("tensor", oproj, reads=["OT", f"wst{cwr}"],
                           writes=["pO0", "pO1", "pO2", "pO3"])
                for t in range(TQ):
                    pp.add("vector", (lambda ds=ds, t=t: lambda e:
                                      e.scalar_tensor_tensor(
                        xs_sb[:, t * 2048 + ds * 512:t * 2048 + (ds + 1) * 512],
                        p_out[:, t * 512:(t + 1) * 512], 1.0,
                        xs_sb[:, t * 2048 + ds * 512:t * 2048 + (ds + 1) * 512],
                        ALU.mult, ALU.add))(),
                        reads=[f"pO{t}", f"xs{t}"], writes=[f"xs{t}"])

            if s == 0 and "tmp" in taps:
                t7 = tapt("tmp", [128, 8192], F32)
                pp.add("gpsimd", lambda e: e.dma_start(out=t7, in_=xs_sb[:]),
                       reads=["xs0", "xs1", "xs2", "xs3"], writes=["OUT"],
                       group="g_tap")

            if maxphase < 6:
                continue
            for hc in range(GH):
                par = hc % 2
                wr1 = next_wst()
                pp.add("sync", (lambda hc=hc, wr1=wr1: lambda e: e.dma_start(
                    out=wst_sb[wr1][:],
                    in_=wguT_s[hc * 128:(hc + 1) * 128, :]))(),
                    reads=["wguTs"], writes=[f"wst{wr1}"], group=f"g_wst{wr1}")
                wr2 = next_wst()
                pp.add("sync", (lambda hc=hc, wr2=wr2: lambda e: e.dma_start(
                    out=wst_sb[wr2][:],
                    in_=wguT_s[(64 + hc) * 128:(65 + hc) * 128, :]))(),
                    reads=["wguTs"], writes=[f"wst{wr2}"], group=f"g_wst{wr2}")

                def gmm(e, wr1=wr1):
                    for dc in range(DC):
                        ins = e.matmul(pA[:],
                                       wst_sb[wr1][:, dc * 128:(dc + 1) * 128],
                                       xnT_sb[:, dc * 512:(dc + 1) * 512],
                                       start=(dc == 0), stop=(dc == DC - 1))
                    return ins
                pp.add("tensor", gmm, reads=[f"wst{wr1}", "xnT"], writes=["pA"])
                pp.add("scalar", (lambda par=par: lambda e: e.activation(
                    As_sb[0][:], pA[:], AF.Silu))(),
                    reads=["pA"], writes=["As0"])

                def umm(e, wr2=wr2):
                    for dc in range(DC):
                        ins = e.matmul(pB[:],
                                       wst_sb[wr2][:, dc * 128:(dc + 1) * 128],
                                       xnT_sb[:, dc * 512:(dc + 1) * 512],
                                       start=(dc == 0), stop=(dc == DC - 1))
                    return ins
                pp.add("tensor", umm, reads=[f"wst{wr2}", "xnT"], writes=["pB"])
                pp.add("vector", (lambda hc=hc, par=par: lambda e: e.tensor_tensor(
                    big_sb[:, hc * 512:(hc + 1) * 512], As_sb[0][:], pB[:],
                    ALU.mult))(),
                    reads=["As0", "pB"], writes=["H"])
                pp.add("scalar", (lambda hc=hc, par=par: lambda e: e.activation(
                    hsq_sb[0][:], big_sb[:, hc * 512:(hc + 1) * 512],
                    AF.Square))(),
                    reads=["H"], writes=["hsq0"])
                pp.add("tensor", (lambda hc=hc, par=par: lambda e: e.matmul(
                    pC[:], ones_sb[:], hsq_sb[0][:],
                    start=(hc == 0), stop=(hc == GH - 1)))(),
                    reads=["hsq0", CONST], writes=["pC"])
            pp.add("scalar", lambda e: e.activation(
                rbb_sb[:], pC[:], AF.Sqrt, bias=eps_ff[:], scale=1.0 / 8192.0),
                reads=["pC", CONST], writes=["rbb"])
            pp.add("vector", lambda e: e.reciprocal(rkb_sb[:], rbb_sb[:]),
                   reads=["rbb"], writes=["rkb"])

            def invtr(e):
                for t in range(TQ):
                    ins = e.transpose(pA[:, t * 128:(t + 1) * 128],
                                      rkb_sb[:, t * 128:(t + 1) * 128],
                                      identf_sb[:])
                return ins
            pp.add("tensor", invtr, reads=["rkb", CONST], writes=["pA"])

            def invcol(e):
                for t in range(TQ):
                    ins = e.tensor_scalar_mul(invc_sb[:, t:t + 1],
                                              pA[:, t * 128:t * 128 + 1], 1.0)
                return ins
            pp.add("vector", invcol, reads=["pA"], writes=["invc"])

            if s == 0 and "H" in taps:
                t8 = tapt("H", [128, 1024], BF16)
                pp.add("gpsimd", lambda e: e.dma_start(out=t8,
                                                       in_=big_sb[:, 0:1024]),
                       reads=["H"], writes=["OUT"], group="g_tap")
            if s == 0 and "invc" in taps:
                t9 = tapt("invc", [128, 4], F32)
                pp.add("gpsimd", lambda e: e.dma_start(out=t9, in_=invc_sb[:]),
                       reads=["invc"], writes=["OUT"], group="g_tap")

            if maxphase < 7:
                continue
            for ds in range(DS):
                for hc in range(GH):
                    if hc % 4 == 0:
                        wr = next_wst()
                        pp.add("sync", (lambda ds=ds, hc=hc, wr=wr:
                                        lambda e: e.dma_start(
                            out=wst_sb[wr][:].rearrange("p (b c) -> p b c", c=512),
                            in_=wpT_s[(ds * 64 + hc) * 128:
                                      (ds * 64 + hc + 4) * 128, :].rearrange(
                                "(b p) c -> p b c", p=128)))(),
                            reads=["wpTs"], writes=[f"wst{wr}"],
                            group=f"g_wst{wr}")
                        cwr = wr

                    def fproj(e, hc=hc, wr=cwr):
                        for t in range(TQ):
                            ins = e.matmul(
                                p_out[:, t * 512:(t + 1) * 512],
                                big_sb[:, hc * 512 + t * 128:
                                       hc * 512 + (t + 1) * 128],
                                wst_sb[wr][:, (hc % 4) * 512:(hc % 4 + 1) * 512],
                                start=(hc == 0), stop=(hc == GH - 1))
                        return ins
                    pp.add("tensor", fproj, reads=["H", f"wst{cwr}"],
                           writes=["pO0", "pO1", "pO2", "pO3"])
                for t in range(TQ):
                    ob = 0
                    pp.add("vector", (lambda ds=ds, t=t, ob=ob: lambda e:
                                      e.scalar_tensor_tensor(
                        outb_sb[ob][:],
                        p_out[:, t * 512:(t + 1) * 512], invc_sb[:, t:t + 1],
                        xs_sb[:, t * 2048 + ds * 512:t * 2048 + (ds + 1) * 512],
                        ALU.mult, ALU.add))(),
                        reads=[f"pO{t}", "invc", f"xs{t}"], writes=[f"outb{ob}"])
                    pp.add("gpsimd", (lambda s=s, ds=ds, t=t, ob=ob:
                                      lambda e: e.dma_start(
                        out=out_d[s * TS + t * 128:s * TS + (t + 1) * 128,
                                  ds * 512:(ds + 1) * 512],
                        in_=outb_sb[ob][:]))(),
                        reads=[f"outb{ob}"], writes=["OUT"],
                        group=f"g_out{ob}")

        pp.add("vector", lambda e: e.engine_nop(), reads=["OUT"])
        pp.emit_block()

    return nc, dbg


_CACHE = {}


def kernel(L0_attn_x, L0_attn_wq, L0_attn_wk, L0_attn_wv, L0_attn_wo,
           L0_attn_q_gamma, L0_attn_k_gamma, L0_ffn_w_gate_up, L0_ffn_w_proj):
    if "nc" not in _CACHE:
        _CACHE["nc"] = build()[0]
    nc = _CACHE["nc"]
    x = np.asarray(L0_attn_x, dtype=np.float32)
    shared = {
        "wq": np.ascontiguousarray(np.asarray(L0_attn_wq, np.float32)),
        "wk": np.ascontiguousarray(np.asarray(L0_attn_wk, np.float32)),
        "wv": np.ascontiguousarray(np.asarray(L0_attn_wv, np.float32)),
        "wo": np.ascontiguousarray(np.asarray(L0_attn_wo, np.float32)),
        "qg": np.ascontiguousarray(np.asarray(L0_attn_q_gamma, np.float32)),
        "kg": np.ascontiguousarray(np.asarray(L0_attn_k_gamma, np.float32)),
        "wgu": np.ascontiguousarray(np.asarray(L0_ffn_w_gate_up, np.float32)),
        "wp": np.ascontiguousarray(np.asarray(L0_ffn_w_proj, np.float32)),
    }
    in_maps = []
    for b in range(8):
        m = dict(shared)
        m["x"] = np.ascontiguousarray(x[b])
        in_maps.append(m)
    res = run_bass_kernel_spmd(nc, in_maps, core_ids=list(range(8)))
    return np.stack([np.asarray(res.results[b]["out"], np.float32)
                     for b in range(8)], axis=0)
